# version 10
# baseline (speedup 1.0000x reference)
import numpy as np
import ml_dtypes
import contextlib
import concourse.bass as bass
import concourse.mybir as mybir
from concourse.bass_utils import run_bass_kernel_spmd

F32 = mybir.dt.float32
BF16 = mybir.dt.bfloat16
AF = mybir.ActivationFunctionType
ALU = mybir.AluOpType
AX = mybir.AxisListType.X

P = 128
C = 1024
TT = 8
DT = 16
NST = 16
ICT = 6
E = 32
EPS = 1e-5

ENGINES = ("t", "s", "v", "y", "g")


class Builder:
    def __init__(self):
        self.ev = {e: [] for e in ENGINES}
        self.cnt = {}
        self.ms = {}

    def newsem(self, name):
        self.cnt[name] = 0

    CHAIN = {"v": "Vm", "s": "Sm"}

    def op(self, eng, fn, waits=(), inc=None, inc_n=1):
        w = [(sn, v) for (sn, v) in waits if v > 0]
        ch = self.CHAIN.get(eng)
        if ch is not None:
            if self.cnt[ch] > 0:
                w.append((ch, self.cnt[ch]))
            if inc is None:
                inc, inc_n = ch, 1
            else:
                assert inc == ch
        if inc is not None:
            if inc_n == 16 and self.cnt[inc] > 0:
                w.append((inc, self.cnt[inc]))
            self.cnt[inc] += inc_n
        self.ev[eng].append((tuple(w), fn, inc, inc_n))
        return self.cnt[inc] if inc is not None else None

    def c(self, name):
        return self.cnt[name]


def build_nc(do_p0=True, do_mamba=True, do_ffn=True, do_p1=True, do_router=True, do_out=True, p1_level=9):
    nc = bass.Bass("TRN2", target_bir_lowering=False, debug=False)

    xm_d = nc.dram_tensor("xm", [2048, 2048], F32, kind="ExternalInput").ap()
    xp_d = nc.dram_tensor("xq", [2048, 2048], F32, kind="ExternalInput").ap()
    wg_d = nc.dram_tensor("wg", [E, ICT, 128, 2048], F32, kind="ExternalInput").ap()
    wu_d = nc.dram_tensor("wu", [E, ICT, 128, 2048], F32, kind="ExternalInput").ap()
    wd_d = nc.dram_tensor("wd", [E, 8, 128, 1536], F32, kind="ExternalInput").ap()
    wxT_d = nc.dram_tensor("wxT", [128, 16 * 96], F32, kind="ExternalInput").ap()
    rwT_d = nc.dram_tensor("rwT", [128, 16 * 32], F32, kind="ExternalInput").ap()
    wdtT_d = nc.dram_tensor("wdtT", [64, 2048], F32, kind="ExternalInput").ap()
    alog_d = nc.dram_tensor("alog", [128, 256], F32, kind="ExternalInput").ap()
    bdt_d = nc.dram_tensor("bdt", [128, 16], F32, kind="ExternalInput").ap()
    dsk_d = nc.dram_tensor("dsk", [128, 16], F32, kind="ExternalInput").ap()
    id_d = nc.dram_tensor("idn", [128, 128], F32, kind="ExternalInput").ap()
    onb_d = nc.dram_tensor("onb", [1, 128], BF16, kind="ExternalInput").ap()
    out_d = nc.dram_tensor("out", [2048, 2048], F32, kind="ExternalOutput").ap()
    import os
    _dbg = os.environ.get("KDBG_CARRY") == "1"
    dbgc_d = (nc.dram_tensor("dbg_carry", [128, DT * NST], F32,
                             kind="ExternalOutput").ap() if _dbg else None)
    rows_d = nc.dram_tensor("rows_scr", [96, 1024], BF16).ap()
    xnp_d = [nc.dram_tensor(f"xnp{i}", [128, DT * C], BF16).ap() for i in range(2)]

    es = contextlib.ExitStack()
    sb = lambda n_, s, d: es.enter_context(nc.sbuf_tensor(n_, s, d))[:]
    pst = lambda n_, s: es.enter_context(nc.psum_tensor(n_, s, F32))[:]

    acc = sb("acc", [P, DT * C], F32)
    xnT = sb("xnT", [P, DT * C], BF16)
    wgs = [sb(f"wgs{i}", [P, 16 * 128], BF16) for i in range(2)]
    wus = [sb(f"wus{i}", [P, 16 * 128], BF16) for i in range(2)]
    wds = [sb(f"wds{i}", [P, 768], BF16) for i in range(3)]
    harena = sb("harena", [P, 2 * ICT * C], BF16)
    wb = sb("wb", [P, 2 * C], BF16)
    g_sb = sb("g_sb", [P, 1024], BF16)
    u_sb = sb("u_sb", [P, 1024], BF16)
    sg_sb = sb("sg_sb", [P, 512], BF16)
    B_sb = sb("B_sb", [P, 2 * C], BF16)
    C_sb = sb("C_sb", [P, 2 * C], BF16)
    abar = sb("abar", [P, 2 * C], F32)
    bx = sb("bx", [P, C], BF16)
    hsc = sb("hsc", [P, C], BF16)
    ybuf = sb("ybuf", [P, C], F32)
    dxt = sb("dxt", [P, C], BF16)
    delta = sb("delta", [P, C], BF16)
    x_tile = sb("x_tile", [P, 2048], F32)
    out_tile = sb("out_tile", [P, C], F32)
    dscr = out_tile[:, 0:512]
    xf32 = [sb(f"xf32_{i}", [P, 128], F32) for i in range(2)]
    xpT = sb("xpT", [96, C], BF16)
    xpT_p = [out_tile[:, 512:1024].bitcast(BF16)[0:96, :],
             sb("xpT_p1", [96, C], BF16)[0:96, :]]
    carry = sb("carry", [P, DT * NST], F32)
    A_sb = sb("A_sb", [P, DT * NST], F32)
    bdt_sb = sb("bdt_sb", [P, 16], F32)
    dsk_sb = sb("dsk_sb", [P, 16], F32)
    wxT = sb("wxT_s", [P, 16 * 96], BF16)
    rwT = sb("rwT_s", [P, 16 * 32], F32)
    wdtT = sb("wdtT_s", [64, 2048], BF16)
    wT = sb("wT", [32, 128], BF16)
    ident = sb("ident", [P, 128], F32)
    st24 = sb("st24", [P, 24], F32)
    mv = sb("mv", [P, 2], F32)
    nmean = sb("nmean", [P, 1], F32)
    rstd = sb("rstd", [P, 2], F32)
    q_sb = sb("q_sb", [P, 32], F32)
    q_cur = sb("q_cur", [P, 32], F32)
    msk = sb("msk", [P, 32], F32)
    selw = sb("selw", [P, 32], F32)
    w_t = sb("w_t", [P, 32], F32)
    negm = sb("negm", [P, 1], F32)
    accs = sb("accs", [P, 1], F32)
    mk = sb("mk", [P, 1], F32)
    rcp = sb("rcp", [P, 1], F32)
    cst = sb("cst", [P, 3], F32)

    pb_g = pst("pb_g", [P, 512])
    pb_u = pst("pb_u", [P, 512])
    pb_a0 = pst("pb_a0", [P, 512])
    pb_a1 = pst("pb_a1", [P, 512])
    pb_b0 = pst("pb_b0", [P, 512])
    pb_b1 = pst("pb_b1", [P, 512])
    pb_dl = pst("pb_dl", [P, 512])
    pb_sm = pst("pb_sm", [P, 512])
    pb_acc = [pb_a0, pb_a1]
    pb_bc = [pb_b0, pb_b1]

    b = Builder()
    for s_ in ("Tm", "Vm", "Sm", "WG0", "WG1", "WU0", "WU1", "WD0", "WD1",
               "WD2", "XT", "BR", "BW", "OUT", "PR", "PRG", "RS", "XNP"):
        b.newsem(s_)

    ms = b.ms


    def mm(out_ap, lhs_ap, rhs_ap, start, stop, waits=(), inc=None):
        return b.op("t", lambda t, o=out_ap, l=lhs_ap, r=rhs_ap, s1=start,
                    s2=stop: t.matmul(o, l, r, start=s1, stop=s2,
                                      skip_group_check=True),
                    waits, inc)

    def startup():
        loads = [(A_sb, alog_d), (bdt_sb, bdt_d), (dsk_sb, dsk_d),
                 (rwT, rwT_d), (ident, id_d)]
        for dst, src in loads:
            b.op("y", lambda y, d=dst, s=src: y.dma_start(d, s), inc="PR", inc_n=16)
        b.op("g", lambda g: g.dma_start(wdtT, wdtT_d), inc="PRG", inc_n=16)
        b.op("g", lambda g: g.dma_start(wxT, wxT_d), inc="PRG", inc_n=16)
        pr = b.c("PR")
        prg = b.c("PRG")
        b.op("s", lambda s: s.activation(A_sb, A_sb, AF.Exp),
             [("PR", pr), ("PRG", prg)], "Sm")
        sA = b.c("Sm")
        b.op("v", lambda v: v.tensor_scalar_mul(A_sb, A_sb, -1.0),
             [("Sm", sA), ("PR", pr)], "Vm")
        b.op("v", lambda v: v.memset(carry, 0.0), [], "Vm")
        b.op("v", lambda v: v.memset(cst[:, 0:1], EPS), [], "Vm")
        b.op("v", lambda v: v.memset(cst[:, 1:2], 1.0), [], "Vm")
        b.op("v", lambda v: v.memset(cst[:, 2:3], -0.5), [], "Vm")
        ms["startup_v"] = b.c("Vm")
        ms["params"] = pr

    def p1(chunk_tag, xsrc, router):
        guardT = b.c("Tm")
        for tt in range(TT):
            b.op("y", lambda y, a=xsrc(tt): y.dma_start(x_tile, a),
                 [("Vm", ms.get("x_tile_free", 0)), ("Tm", ms.get("x_tile_freeT", 0))],
                 "XT", 16)
            xt = b.c("XT")
            b.op("v", lambda v: [v.bn_stats(st24[:, i * 6:(i + 1) * 6],
                                            x_tile[:, i * 512:(i + 1) * 512])
                                 for i in range(4)][-1], [("XT", xt)], "Vm")
            b.op("v", lambda v: v.bn_aggr(mv, st24), [], "Vm")
            b.op("v", lambda v: v.tensor_scalar_mul(nmean, mv[:, 0:1], -1.0),
                 [], "Vm")
            svm = b.c("Vm")
            b.op("s", lambda s: s.activation(rstd[:, 0:1], mv[:, 1:2], AF.Ln,
                                             bias=cst[:, 0:1]),
                 [("Vm", svm), ("Vm", ms["startup_v"])], "Sm")
            b.op("s", lambda s: s.activation(rstd[:, 1:2], rstd[:, 0:1], AF.Exp,
                                             scale=cst[:, 2:3]), [], "Sm")
            srs = b.c("Sm")
            if p1_level < 2:
                continue
            b.op("v", lambda v: v.tensor_scalar_add(x_tile, x_tile,
                                                    nmean[:, 0:1]), [], "Vm")
            b.op("v", lambda v: v.tensor_scalar_mul(x_tile, x_tile,
                                                    rstd[:, 1:2]),
                 [("Sm", srs)], "Vm")
            vnorm = b.c("Vm")
            if p1_level < 3:
                continue
            for dc in range(DT):
                sl = pb_bc[dc % 2][:, 0:128]
                w_ = [("Vm", ms.get(f"bcV{dc % 2}", 0)),
                      ("Sm", ms.get(f"bcS{dc % 2}", 0))]
                b.op("t", lambda t, o=sl, i_=x_tile[:, dc * 128:(dc + 1) * 128]:
                     t.transpose(o, i_, ident), w_ + [("Vm", vnorm)], "Tm")
                tp = b.c("Tm")
                b.op("s", lambda s, s_=sl, d=xf32[dc % 2]: s.copy(d, s_),
                     [("Tm", tp), ("Vm", ms.get(f"xf32V{dc % 2}", 0))], "Sm")
                ms[f"bcS{dc % 2}"] = b.c("Sm")
                sf = b.c("Sm")
                b.op("v", lambda v, s_=xf32[dc % 2],
                     d=xnT[:, dc * C + tt * 128: dc * C + tt * 128 + 128]:
                     v.tensor_copy(d, s_),
                     [("Sm", sf), ("Tm", guardT), ("XNP", b.c("XNP"))], "Vm")
                ms[f"xf32V{dc % 2}"] = b.c("Vm")
                ms[f"bcV{dc % 2}"] = b.c("Vm")
                if p1_level < 5:
                    continue
                mm(pb_sm[0:96, 0:128], wxT[:, dc * 96:(dc + 1) * 96],
                   xnT[:, dc * C + tt * 128: dc * C + tt * 128 + 128],
                   dc == 0, dc == DT - 1,
                   [("Vm", b.c("Vm")), ("Sm", ms.get("xp_evac", 0)),
                    ("Sm", ms.get("wT_evac", 0)),
                    ("PRG", b.c("PRG"))],
                   "Tm" if dc == DT - 1 else None)
                if router:
                    mm(pb_dl[:, 0:32], xf32[dc % 2],
                       rwT[:, dc * 32:(dc + 1) * 32], dc == 0, dc == DT - 1,
                       [("Sm", sf), ("Sm", ms.get("pbdl_S", 0)),
                        ("Vm", ms.get("pbdl_V", 0))],
                       "Tm" if dc == DT - 1 else None)
            tmm = b.c("Tm")
            ms["x_tile_freeT"] = tmm
            ms["x_tile_free"] = b.c("Vm")
            if p1_level < 6:
                continue
            b.op("s", lambda s, d=xpT[:, tt * 128:(tt + 1) * 128]:
                 s.copy(d, pb_sm[0:96, 0:128]),
                 [("Tm", tmm), ("BR", b.c("BR")), ("RS", b.c("RS")),
                  ("Vm", ms.get("xpT_vfree", 0))], "Sm")
            ms["xp_evac"] = b.c("Sm")
            if router:
                lg = pb_dl[:, 0:32]
                b.op("v", lambda v: v.reduce_max(negm, lg, axis=AX, negate=True),
                     [("Tm", tmm)], "Vm")
                vv = b.c("Vm")
                ms["pbdl_V"] = vv
                b.op("s", lambda s: s.activation(q_sb, lg, AF.Exp,
                                                 bias=negm[:, 0:1]),
                     [("Vm", vv)], "Sm")
                ms["pbdl_S"] = b.c("Sm")
                sq = b.c("Sm")
                b.op("v", lambda v: v.tensor_copy(q_cur, q_sb), [("Sm", sq)], "Vm")
                for k in range(8):
                    b.op("v", lambda v: v.reduce_max(mk, q_cur, axis=AX), [], "Vm")
                    if k == 0:
                        b.op("v", lambda v: v.tensor_copy(accs, mk), [], "Vm")
                    else:
                        b.op("v", lambda v: v.tensor_add(accs, accs, mk), [], "Vm")
                    b.op("v", lambda v: v.tensor_single_scalar(msk, q_cur,
                                                               mk[:, 0:1],
                                                               ALU.is_ge), [], "Vm")
                    b.op("v", lambda v: v.scalar_tensor_tensor(q_cur, msk, -2.0,
                                                               q_cur, ALU.mult,
                                                               ALU.add), [], "Vm")
                b.op("v", lambda v: v.reciprocal(rcp, accs), [], "Vm")
                b.op("v", lambda v: v.tensor_sub(selw, q_sb, q_cur), [], "Vm")
                b.op("v", lambda v: v.tensor_scalar(selw, selw, 0.5, None,
                                                    ALU.mult), [], "Vm")
                b.op("v", lambda v: v.tensor_scalar_mul(w_t, q_sb, rcp[:, 0:1]),
                     [], "Vm")
                b.op("v", lambda v: v.tensor_mul(w_t, w_t, selw), [], "Vm")
                vw = b.c("Vm")
                b.op("t", lambda t: t.transpose(pb_sm[0:32, 256:384], w_t,
                                                ident),
                     [("Vm", vw), ("Sm", ms.get("wT_evac", 0))], "Tm")
                twt = b.c("Tm")
                b.op("s", lambda s: s.copy(wT, pb_sm[0:32, 256:384]),
                     [("Tm", twt), ("RS", ms.get("wT_rs", 0))], "Sm")
                ms["wT_evac"] = b.c("Sm")
                swt = b.c("Sm")
                b.op("y", lambda y, d=rows_d[32:64, tt * 128:(tt + 1) * 128]:
                     y.dma_start(d, wT), [("Sm", swt)], "RS", 16)
                ms["wT_rs"] = b.c("RS")
        ms[f"xnT_done_{chunk_tag}"] = b.c("Vm")
        ms[f"xp_done_{chunk_tag}"] = b.c("Sm")
        if p1_level < 7:
            return
        sx = b.c("Sm")
        if router:
            b.op("y", lambda y: y.dma_start(rows_d[0:32, :], xpT[64:96, :]),
                 [("Sm", sx), ("BR", b.c("BR")), ("BW", b.c("BW"))], "RS", 16)
        else:
            sub = int(chunk_tag[1])
            b.op("v", lambda v, d=xpT_p[sub]: v.tensor_copy(d, xpT),
                 [("Sm", sx)], "Vm")
            ms[f"xpp_ready_{chunk_tag}"] = b.c("Vm")
            ms["xpT_vfree"] = b.c("Vm")
            b.op("y", lambda y, r0=64 + 16 * sub:
                 y.dma_start(rows_d[r0:r0 + 16, :], xpT[64:80, :]),
                 [("Sm", sx), ("BR", b.c("BR"))], "RS", 16)
            vx = b.c("Vm")
            b.op("y", lambda y, d=xnp_d[sub]: y.dma_start(d, xnT),
                 [("Vm", vx)], "XNP", 16)
        ms[f"rows_{chunk_tag}"] = b.c("RS")

    def mamba_prologue(chunk_tag, dt):
        if chunk_tag[0] == "p":
            xp_src = xpT_p[int(chunk_tag[1])]
        else:
            xp_src = xpT
        for hf in range(2):
            mm(pb_dl, wdtT[:, dt * 128:(dt + 1) * 128],
               xp_src[0:64, hf * 512:(hf + 1) * 512], True, True,
               [("Sm", ms.get("pbdl_S", 0)), ("Vm", ms.get("pbdl_V", 0)),
                ("PRG", b.c("PRG")),
                ("Sm", ms.get(f"xp_done_{chunk_tag}", 0)),
                ("Vm", ms.get(f"xpp_ready_{chunk_tag}", 0))], "Tm")
            tdl = b.c("Tm")
            b.op("s", lambda s, bia=bdt_sb[:, dt:dt + 1]:
                 s.activation(dscr, pb_dl, AF.Exp, bias=bia),
                 [("Tm", tdl), ("Vm", ms.get("dl_war", 0)),
                  ("OUT", ms.get("out_dma", 0)),
                  ("Vm", ms.get("out_tile_vfree", 0))], "Sm")
            ms["pbdl_S"] = b.c("Sm")
            b.op("s", lambda s, d=delta[:, hf * 512:(hf + 1) * 512]:
                 s.activation(d, dscr, AF.Ln, bias=cst[:, 1:2]), [], "Sm")
        sdl = b.c("Sm")
        if chunk_tag[0] == "p":
            sub = int(chunk_tag[1])
            b.op("y", lambda y, s_=xnp_d[sub][:, dt * C:(dt + 1) * C]:
                 y.dma_start(dxt, s_),
                 [("Vm", b.c("Vm")), ("XNP", 16 * (sub + 1))], "XT", 16)
            xl = b.c("XT")
            b.op("v", lambda v: v.tensor_mul(dxt, delta, dxt),
                 [("Sm", sdl), ("XT", xl)], "Vm")
        else:
            b.op("v", lambda v, xs=xnT[:, dt * C:(dt + 1) * C]:
                 v.tensor_mul(dxt, delta, xs), [("Sm", sdl)], "Vm")
        ms["dl_war"] = b.c("Vm")

    def mamba_nstep(chunk_tag, dt, n, do_y):
        if True:
            ci = dt * NST + n
            rb = 0 if chunk_tag[0] == "c" else 64 + 16 * int(chunk_tag[1])
            sl_ = n % 2
            b.op("y", lambda y, n_=rb + n, s2=sl_:
                 y.dma_start(B_sb[:, s2 * C:(s2 + 1) * C],
                             rows_d[n_:n_ + 1, :].broadcast_to([P, C])),
                 [("Vm", ms.get(f"Bw{sl_}", 0)),
                  ("RS", ms.get(f"rows_{chunk_tag}", 0))], "BR", 16)
            br = b.c("BR")
            if do_y:
                b.op("y", lambda y, n_=16 + n, s2=sl_:
                     y.dma_start(C_sb[:, s2 * C:(s2 + 1) * C],
                                 rows_d[n_:n_ + 1, :].broadcast_to([P, C])),
                     [("Vm", ms.get(f"Cw{sl_}", 0)),
                      ("RS", ms.get(f"rows_{chunk_tag}", 0))], "BW", 16)
            brc = b.c("BW")
            b.op("s", lambda s, i=ci, s2=sl_: s.activation(
                abar[:, s2 * C:(s2 + 1) * C], delta,
                AF.Exp, scale=A_sb[:, i:i + 1]),
                 [("Vm", ms.get(f"abar_war{sl_}", 0))], "Sm")
            sA = b.c("Sm")
            b.op("v", lambda v, s2=sl_: v.tensor_mul(
                bx, dxt, B_sb[:, s2 * C:(s2 + 1) * C]), [("BR", br)], "Vm")
            ms[f"Bw{sl_}"] = b.c("Vm")
            b.op("v", lambda v, i=ci, s2=sl_: v.tensor_tensor_scan(
                hsc, abar[:, s2 * C:(s2 + 1) * C], bx, carry[:, i:i + 1],
                ALU.mult, ALU.add), [("Sm", sA)], "Vm")
            ms[f"abar_war{sl_}"] = b.c("Vm")
            b.op("v", lambda v, i=ci: v.tensor_copy(carry[:, i:i + 1],
                                                    hsc[:, C - 1:C]), [], "Vm")
            if do_y:
                if n == 0:
                    b.op("v", lambda v, s2=sl_: v.tensor_mul(
                        ybuf, hsc, C_sb[:, s2 * C:(s2 + 1) * C]),
                        [("BW", brc)], "Vm")
                else:
                    b.op("v", lambda v, s2=sl_: v.tensor_mul(
                        bx, hsc, C_sb[:, s2 * C:(s2 + 1) * C]),
                        [("BW", brc)], "Vm")
                    b.op("v", lambda v: v.tensor_add(ybuf, ybuf, bx), [], "Vm")
                ms[f"Cw{sl_}"] = b.c("Vm")

    def mamba_epilogue(dt):
        b.op("v", lambda v, xs=xnT[:, dt * C:(dt + 1) * C],
             dc_=dsk_sb[:, dt:dt + 1]: v.scalar_tensor_tensor(
            ybuf, xs, dc_, ybuf, ALU.mult, ALU.add), [], "Vm")
        b.op("v", lambda v, d=acc[:, dt * C:(dt + 1) * C]:
             v.tensor_add(d, d, ybuf), [], "Vm")
        ms["mamba_v"] = b.c("Vm")

    def mamba_units(chunk_tag, dt, do_y):
        u = [lambda: mamba_prologue(chunk_tag, dt)]
        for n in range(NST):
            u.append(lambda n=n: mamba_nstep(chunk_tag, dt, n, do_y))
        if do_y:
            u.append(lambda: mamba_epilogue(dt))
        return u

    def mamba_dt(chunk_tag, dt, do_y):
        for u in mamba_units(chunk_tag, dt, do_y):
            u()

    wload_idx = {"wg": 0, "wu": 0, "wd": 0}

    def load_gu_col(nm, dr, slots, e, ic):
        k = wload_idx[nm]
        slot = k % 2
        wload_idx[nm] = k + 1
        sem = ("WG" if nm == "wg" else "WU") + str(slot)
        freekey = f"free_{nm}{slot}"
        b.op("g", lambda g, s_=slots[slot], a=dr[e, ic]: g.dma_start(s_, a),
             [("Tm", ms.get(freekey, 0))], sem, 16)
        return slot, b.c(sem)

    def load_wd_col(e, hc):
        k = wload_idx["wd"]
        slot = k % 3
        wload_idx["wd"] = k + 1
        sem = f"WD{slot}"
        b.op("g", lambda g, s_=wds[slot],
             a=wd_d[e, hc // 2, :, (hc % 2) * 768:(hc % 2) * 768 + 768]:
             g.dma_start(s_, a),
             [("Tm", ms.get(f"free_wd{slot}", 0))], sem, 16)
        return slot, b.c(sem)

    def pair(chunk_tag, p_idx, extra=None):
        extra = list(extra or [])
        n_slots = 2 * ICT + 16
        ne = len(extra)
        drip_sched = []
        for i in range(n_slots):
            if i < 2 * ICT:
                drip_sched.append((ne * 3 * (i + 1)) // (10 * 2 * ICT))
            else:
                base = (ne * 3) // 10
                drip_sched.append(base + ((ne - base) * (i - 2 * ICT + 1))
                                  // 16)
        slot_i = [0]

        def drip():
            want = drip_sched[min(slot_i[0], n_slots - 1)]
            slot_i[0] += 1
            while extra and (len(drip_sched) - len(extra) >= 0):
                done = drip_sched[-1] - len(extra)
                if done < want:
                    extra.pop(0)()
                else:
                    break

        e0, e1 = 2 * p_idx, 2 * p_idx + 1
        xn_ok = ms.get(f"xnT_done_{chunk_tag}", 0)
        for ei, e in enumerate((e0, e1)):
            b.op("y", lambda y, e_=e, i=ei: y.dma_start(
                wb[:, i * C:(i + 1) * C],
                rows_d[32 + e_:33 + e_, :].broadcast_to([P, C])),
                [("Vm", ms.get("wb_war", 0)),
                 ("RS", ms.get(f"rows_{chunk_tag}", 0))], "BW", 16)
        swb = b.c("BW")
        for ei, e in enumerate((e0, e1)):
            for ic in range(ICT):
                gslot, gsem = load_gu_col("wg", wg_d, wgs, e, ic)
                uslot, usem = load_gu_col("wu", wu_d, wus, e, ic)
                for sl in range(2):
                    j2 = (ic * 2 + sl) % 2
                    pg = [pb_g, pb_b0][0]
                    pu = [pb_u, pb_b1][0]
                    for hc in range(16):
                        w_ = [(f"WG{gslot}", gsem), ("Vm", xn_ok)]
                        if hc == 0:
                            w_.append(("Sm", ms.get("g_evac0", 0)))
                        mm(pg[:], wgs[gslot][:, hc * 128:(hc + 1) * 128],
                           xnT[:, hc * C + sl * 512: hc * C + (sl + 1) * 512],
                           hc == 0, hc == 15, w_, "Tm" if hc == 15 else None)
                    tg = b.c("Tm")
                    if sl == 1:
                        ms[f"free_wg{gslot}"] = tg
                    for hc in range(16):
                        w_ = [(f"WU{uslot}", usem)]
                        if hc == 0:
                            w_.append(("Sm", ms.get("u_evac0", 0)))
                        mm(pu[:], wus[uslot][:, hc * 128:(hc + 1) * 128],
                           xnT[:, hc * C + sl * 512: hc * C + (sl + 1) * 512],
                           hc == 0, hc == 15, w_, "Tm" if hc == 15 else None)
                    tu = b.c("Tm")
                    if sl == 1:
                        ms[f"free_wu{uslot}"] = tu
                    b.op("s", lambda s, j=j2, p_=pg: s.copy(
                        g_sb[:, j * 512:(j + 1) * 512], p_[:]),
                        [("Tm", tg), ("Sm", ms.get(f"silu{j2}", 0))], "Sm")
                    ms["g_evac0"] = b.c("Sm")
                    b.op("s", lambda s, j=j2, p_=pu: s.copy(
                        u_sb[:, j * 512:(j + 1) * 512], p_[:]),
                        [("Tm", tu), ("Vm", ms.get(f"h1_{j2}", 0))], "Sm")
                    ms["u_evac0"] = b.c("Sm")
                    b.op("s", lambda s, j=j2: s.activation(
                        sg_sb[:, 0:512],
                        g_sb[:, j * 512:(j + 1) * 512], AF.Silu),
                        [("Vm", ms.get("sg_war", 0))], "Sm")
                    ms[f"silu{j2}"] = b.c("Sm")
                    ssl = b.c("Sm")
                    b.op("v", lambda v, j=j2: v.tensor_mul(
                        sg_sb[:, 0:512],
                        sg_sb[:, 0:512],
                        u_sb[:, j * 512:(j + 1) * 512]), [("Sm", ssl)], "Vm")
                    ms[f"h1_{j2}"] = b.c("Vm")
                    b.op("v", lambda v, j=j2, i=ei, c_=ic, s_=sl: v.tensor_mul(
                        harena[:, i * ICT * C + c_ * C + s_ * 512:
                               i * ICT * C + c_ * C + s_ * 512 + 512],
                        sg_sb[:, 0:512],
                        wb[:, i * C + s_ * 512: i * C + s_ * 512 + 512]),
                        [("BW", swb), ("Tm", ms.get("h_rdT", 0))], "Vm")
                    ms["sg_war"] = b.c("Vm")
                drip()
            ms[f"h_done{ei}"] = b.c("Vm")
        ms["wb_war"] = b.c("Vm")
        hdone = b.c("Vm")
        for hc in range(16):
            s0, d0 = load_wd_col(e0, hc)
            s1, d1 = load_wd_col(e1, hc)
            for sl in range(2):
                k = (hc * 2 + sl) % 2
                nmm = 0
                for ei, (e, ws_, dsem) in enumerate(
                        ((e0, s0, d0), (e1, s1, d1))):
                    for icc in range(ICT):
                        w_ = [(f"WD{ws_}", dsem), ("Vm", hdone)]
                        if nmm == 0:
                            w_.append(("Vm", ms.get(f"acc_evac{k}", 0)))
                        mm(pb_acc[k][:],
                           wds[ws_][:, icc * 128:(icc + 1) * 128],
                           harena[:, ei * ICT * C + icc * C + sl * 512:
                                  ei * ICT * C + icc * C + sl * 512 + 512],
                           nmm == 0, nmm == 11, w_,
                           "Tm" if nmm == 11 else None)
                        nmm += 1
                tdn = b.c("Tm")
                if sl == 1:
                    ms[f"free_wd{s0}"] = tdn
                    ms[f"free_wd{s1}"] = tdn
                b.op("v", lambda v, h_=hc, s_=sl, k_=k: v.tensor_add(
                    acc[:, h_ * C + s_ * 512: h_ * C + s_ * 512 + 512],
                    acc[:, h_ * C + s_ * 512: h_ * C + s_ * 512 + 512],
                    pb_acc[k_][:]), [("Tm", tdn)], "Vm")
                ms[f"acc_evac{k}"] = b.c("Vm")
            drip()
        while extra:
            extra.pop(0)()
        ms["h_rdT"] = b.c("Tm")

    def out_assembly(c):
        vacc = b.c("Vm")
        s_dscr = b.c("Sm")
        for tt in range(TT):
            b.op("y", lambda y, t_=tt: y.dma_start(
                x_tile, xm_d[c * C + t_ * 128: c * C + (t_ + 1) * 128, :]),
                [("Vm", ms.get("x_tile_free", 0)),
                 ("Tm", ms.get("x_tile_freeT", 0))], "XT", 16)
            xt = b.c("XT")
            for half in range(2):
                for d8 in range(8):
                    dc = half * 8 + d8
                    sl = pb_bc[d8 % 2][:, 0:128]
                    b.op("t", lambda t, o=sl,
                         i_=acc[:, dc * C + tt * 128: dc * C + tt * 128 + 128]:
                         t.transpose(o, i_, ident),
                         [("Vm", vacc), ("Vm", ms.get(f"bcV{d8 % 2}", 0)),
                          ("Sm", ms.get(f"bcS{d8 % 2}", 0))], "Tm")
                    ttp = b.c("Tm")
                    b.op("v", lambda v, s_=sl, d8_=d8, dc_=dc: v.tensor_add(
                        out_tile[:, d8_ * 128:(d8_ + 1) * 128], s_,
                        x_tile[:, dc_ * 128:(dc_ + 1) * 128]),
                        [("Tm", ttp), ("XT", xt), ("Sm", s_dscr),
                         ("OUT", ms.get("out_dma", 0))], "Vm")
                    ms[f"bcV{d8 % 2}"] = b.c("Vm")
                vout = b.c("Vm")
                b.op("y", lambda y, t_=tt, h_=half: y.dma_start(
                    out_d[c * C + t_ * 128: c * C + (t_ + 1) * 128,
                          h_ * 1024:(h_ + 1) * 1024], out_tile),
                    [("Vm", vout)], "OUT", 16)
                ms["out_dma"] = b.c("OUT")
            ms["x_tile_free"] = b.c("Vm")
            ms["x_tile_freeT"] = b.c("Tm")
            ms["out_tile_vfree"] = b.c("Vm")

    startup()
    if do_p0:
        for sub in range(2):
            p1(f"p{sub}", lambda tt, s_=sub: xp_d[s_ * C + tt * 128:
                                                  s_ * C + (tt + 1) * 128, :],
               False)
    for c in range(2):
        tag = f"c{c}"
        b.op("v", lambda v: v.memset(acc, 0.0),
             [("Tm", ms.get("x_tile_freeT", 0))], "Vm")
        if do_p1:
            p1(tag, lambda tt, c_=c: xm_d[c_ * C + tt * 128:
                                          c_ * C + (tt + 1) * 128, :], do_router)
        if c == 0 and do_p0 and do_mamba and os.environ.get("KP0W") == "0":
            for dt in range(DT):
                mamba_dt("p0", dt, False)
                mamba_dt("p1", dt, False)
        for p_idx in range(16):
            extra = []
            if c == 0 and do_p0 and do_mamba and os.environ.get("KP0W") != "0":
                extra += mamba_units("p0", p_idx, False)
                extra += mamba_units("p1", p_idx, False)
            if do_mamba and os.environ.get("KP0ONLY") != "1":
                extra += mamba_units(tag, p_idx, True)
            if do_ffn:
                pair(tag, p_idx, extra)
            else:
                for u in extra:
                    u()
        if do_out:
            out_assembly(c)
    if _dbg:
        b.op("y", lambda y: y.dma_start(dbgc_d, carry),
             [("Vm", b.c("Vm"))], "OUT", 16)
    b.op("y", lambda y: y.nop(), [("OUT", b.c("OUT"))])

    sems = {}
    for s_ in b.cnt:
        sems[s_] = es.enter_context(nc.semaphore(s_))

    def emit(section, eng_key):
        waited = {}
        for (waits, fn, inc, inc_n) in b.ev[eng_key]:
            for sn, v in waits:
                if v > waited.get(sn, 0):
                    section.wait_ge(sems[sn], v)
                    waited[sn] = v
            inst = fn(section)
            if inc is not None:
                inst.then_inc(sems[inc], inc_n)

    with es, nc.Block() as block:
        @block.sync
        def _(sync):
            emit(sync, "y")

        @block.gpsimd
        def _(gpsimd):
            emit(gpsimd, "g")

        @block.tensor
        def _(tensor):
            emit(tensor, "t")

        @block.scalar
        def _(scalar):
            emit(scalar, "s")

        @block.vector
        def _(vector):
            emit(vector, "v")

    return nc


_NC_CACHE = None


def _get_nc():
    global _NC_CACHE
    if _NC_CACHE is None:
        _NC_CACHE = build_nc()
    return _NC_CACHE


def kernel(L0_attn_x, L0_attn_A_log, L0_attn_D, L0_attn_W_x, L0_attn_W_dt,
           L0_attn_b_dt, L0_ffn_router_w, L0_ffn_w_gate, L0_ffn_w_up,
           L0_ffn_w_down):
    x = np.ascontiguousarray(np.asarray(L0_attn_x, dtype=np.float32))
    A_log = np.asarray(L0_attn_A_log, np.float32)
    Dsk = np.asarray(L0_attn_D, np.float32)
    W_x = np.asarray(L0_attn_W_x, np.float32)
    W_dt = np.asarray(L0_attn_W_dt, np.float32)
    b_dt = np.asarray(L0_attn_b_dt, np.float32)
    rw = np.asarray(L0_ffn_router_w, np.float32)
    wg = np.asarray(L0_ffn_w_gate, np.float32)
    wu = np.asarray(L0_ffn_w_up, np.float32)
    wd = np.asarray(L0_ffn_w_down, np.float32)

    wg_r = np.ascontiguousarray(
        wg.reshape(E, 16, 128, ICT, 128).transpose(0, 3, 2, 1, 4)
    ).reshape(E, ICT, 128, 2048)
    wu_r = np.ascontiguousarray(
        wu.reshape(E, 16, 128, ICT, 128).transpose(0, 3, 2, 1, 4)
    ).reshape(E, ICT, 128, 2048)
    wd_r = np.ascontiguousarray(
        wd.reshape(E, ICT, 128, 16, 128).transpose(0, 3, 2, 1, 4)
    )
    wd_r = np.ascontiguousarray(
        wd_r.reshape(E, 8, 2, 128, 768).transpose(0, 1, 3, 2, 4)
    ).reshape(E, 8, 128, 1536)
    wxT = np.ascontiguousarray(
        W_x.T.reshape(16, 128, 96).transpose(1, 0, 2)).reshape(128, 16 * 96)
    rwT = np.ascontiguousarray(
        rw.T.reshape(16, 128, 32).transpose(1, 0, 2)).reshape(128, 16 * 32)
    wdtT = np.ascontiguousarray(W_dt.T)
    alog = np.ascontiguousarray(
        A_log.reshape(16, 128, 16).transpose(1, 0, 2)).reshape(128, 256)
    bdt = np.ascontiguousarray(b_dt.reshape(16, 128).T)
    dsk = np.ascontiguousarray(Dsk.reshape(16, 128).T)
    ident = np.eye(128, dtype=np.float32)
    onb = np.ones((1, 128), dtype=ml_dtypes.bfloat16)

    shared = dict(wg=wg_r, wu=wu_r, wd=wd_r, wxT=wxT, rwT=rwT, wdtT=wdtT,
                  alog=alog, bdt=bdt, dsk=dsk, idn=ident, onb=onb)
    zeros = np.zeros((2048, 2048), np.float32)
    in_maps = []
    for core in range(8):
        bb, half = core // 2, core % 2
        xm = np.ascontiguousarray(x[bb, half * 2048:(half + 1) * 2048])
        xq = zeros if half == 0 else np.ascontiguousarray(x[bb, 0:2048])
        in_maps.append(dict(xm=xm, xq=xq, **shared))

    nc = _get_nc()
    res = run_bass_kernel_spmd(nc, in_maps, core_ids=list(range(8)))
    outs = [res.results[i]["out"] for i in range(8)]
    full = np.concatenate(outs, axis=0).reshape(4, 4096, 2048)
    return full


# version 11
# speedup vs baseline: 1.0347x; 1.0347x over previous
import numpy as np
import ml_dtypes
import contextlib
import concourse.bass as bass
import concourse.mybir as mybir
from concourse.bass_utils import run_bass_kernel_spmd

F32 = mybir.dt.float32
BF16 = mybir.dt.bfloat16
AF = mybir.ActivationFunctionType
ALU = mybir.AluOpType
AX = mybir.AxisListType.X

P = 128
C = 1024
TT = 8
DT = 16
NST = 16
ICT = 6
E = 32
EPS = 1e-5

ENGINES = ("t", "s", "v", "y", "g")


class Builder:
    def __init__(self):
        self.ev = {e: [] for e in ENGINES}
        self.cnt = {}
        self.ms = {}

    def newsem(self, name):
        self.cnt[name] = 0

    CHAIN = {"v": "Vm", "s": "Sm"}

    def op(self, eng, fn, waits=(), inc=None, inc_n=1):
        w = [(sn, v) for (sn, v) in waits if v > 0]
        ch = self.CHAIN.get(eng)
        if ch is not None:
            if self.cnt[ch] > 0:
                w.append((ch, self.cnt[ch]))
            if inc is None:
                inc, inc_n = ch, 1
            else:
                assert inc == ch
        if inc is not None:
            if inc_n == 16 and self.cnt[inc] > 0:
                w.append((inc, self.cnt[inc]))
            self.cnt[inc] += inc_n
        self.ev[eng].append((tuple(w), fn, inc, inc_n))
        return self.cnt[inc] if inc is not None else None

    def c(self, name):
        return self.cnt[name]


def build_nc(do_p0=True, do_mamba=True, do_ffn=True, do_p1=True, do_router=True, do_out=True, p1_level=9):
    nc = bass.Bass("TRN2", target_bir_lowering=False, debug=False)

    xm_d = nc.dram_tensor("xm", [2048, 2048], F32, kind="ExternalInput").ap()
    xp_d = nc.dram_tensor("xq", [2048, 2048], F32, kind="ExternalInput").ap()
    wg_d = nc.dram_tensor("wg", [E, ICT, 128, 2048], F32, kind="ExternalInput").ap()
    wu_d = nc.dram_tensor("wu", [E, ICT, 128, 2048], F32, kind="ExternalInput").ap()
    wd_d = nc.dram_tensor("wd", [E, 8, 128, 1536], F32, kind="ExternalInput").ap()
    wxT_d = nc.dram_tensor("wxT", [128, 16 * 96], F32, kind="ExternalInput").ap()
    rwT_d = nc.dram_tensor("rwT", [128, 16 * 32], F32, kind="ExternalInput").ap()
    wdtT_d = nc.dram_tensor("wdtT", [64, 2048], F32, kind="ExternalInput").ap()
    alog_d = nc.dram_tensor("alog", [128, 256], F32, kind="ExternalInput").ap()
    bdt_d = nc.dram_tensor("bdt", [128, 16], F32, kind="ExternalInput").ap()
    dsk_d = nc.dram_tensor("dsk", [128, 16], F32, kind="ExternalInput").ap()
    id_d = nc.dram_tensor("idn", [128, 128], F32, kind="ExternalInput").ap()
    onb_d = nc.dram_tensor("onb", [1, 128], BF16, kind="ExternalInput").ap()
    out_d = nc.dram_tensor("out", [2048, 2048], F32, kind="ExternalOutput").ap()
    import os
    _dbg = os.environ.get("KDBG_CARRY") == "1"
    dbgc_d = (nc.dram_tensor("dbg_carry", [128, DT * NST], F32,
                             kind="ExternalOutput").ap() if _dbg else None)
    rows_d = nc.dram_tensor("rows_scr", [96, 1024], BF16).ap()
    xnp_d = [nc.dram_tensor(f"xnp{i}", [128, DT * C], BF16).ap() for i in range(2)]

    es = contextlib.ExitStack()
    sb = lambda n_, s, d: es.enter_context(nc.sbuf_tensor(n_, s, d))[:]
    pst = lambda n_, s: es.enter_context(nc.psum_tensor(n_, s, F32))[:]

    acc = sb("acc", [P, DT * C], F32)
    xnT = sb("xnT", [P, DT * C], BF16)
    wgs = [sb(f"wgs{i}", [P, 16 * 128], BF16) for i in range(2)]
    wus = [sb(f"wus{i}", [P, 16 * 128], BF16) for i in range(2)]
    wds = [sb(f"wds{i}", [P, 768], BF16) for i in range(3)]
    harena = sb("harena", [P, 2 * ICT * C], BF16)
    wb = sb("wb", [P, 2 * C], BF16)
    g_sb = sb("g_sb", [P, 1024], BF16)
    u_sb = sb("u_sb", [P, 1024], BF16)
    sg_sb = sb("sg_sb", [P, 512], BF16)
    B_sb = sb("B_sb", [P, 2 * C], BF16)
    C_sb = sb("C_sb", [P, 2 * C], BF16)
    abar = sb("abar", [P, 2 * C], F32)
    bx = sb("bx", [P, C], BF16)
    hsc = sb("hsc", [P, C], BF16)
    ybuf = sb("ybuf", [P, C], F32)
    dxt = sb("dxt", [P, C], BF16)
    delta = sb("delta", [P, C], BF16)
    x_tile = sb("x_tile", [P, 2048], F32)
    out_tile = sb("out_tile", [P, C], F32)
    dscr = out_tile[:, 0:512]
    xf32 = [sb(f"xf32_{i}", [P, 128], F32) for i in range(2)]
    xpT = sb("xpT", [96, C], BF16)
    xpT_p = [out_tile[:, 512:1024].bitcast(BF16)[0:96, :],
             sb("xpT_p1", [96, C], BF16)[0:96, :]]
    carry = sb("carry", [P, DT * NST], F32)
    A_sb = sb("A_sb", [P, DT * NST], F32)
    bdt_sb = sb("bdt_sb", [P, 16], F32)
    dsk_sb = sb("dsk_sb", [P, 16], F32)
    wxT = sb("wxT_s", [P, 16 * 96], BF16)
    rwT = sb("rwT_s", [P, 16 * 32], F32)
    wdtT = sb("wdtT_s", [64, 2048], BF16)
    wT = sb("wT", [32, 128], BF16)
    ident = sb("ident", [P, 128], F32)
    st24 = sb("st24", [P, 24], F32)
    mv = sb("mv", [P, 2], F32)
    nmean = sb("nmean", [P, 1], F32)
    rstd = sb("rstd", [P, 2], F32)
    q_sb = sb("q_sb", [P, 32], F32)
    q_cur = sb("q_cur", [P, 32], F32)
    msk = sb("msk", [P, 32], F32)
    selw = sb("selw", [P, 32], F32)
    w_t = sb("w_t", [P, 32], F32)
    negm = sb("negm", [P, 1], F32)
    accs = sb("accs", [P, 1], F32)
    mk = sb("mk", [P, 1], F32)
    rcp = sb("rcp", [P, 1], F32)
    cst = sb("cst", [P, 3], F32)

    pb_g = pst("pb_g", [P, 512])
    pb_u = pst("pb_u", [P, 512])
    pb_a0 = pst("pb_a0", [P, 512])
    pb_a1 = pst("pb_a1", [P, 512])
    pb_b0 = pst("pb_b0", [P, 512])
    pb_b1 = pst("pb_b1", [P, 512])
    pb_dl = pst("pb_dl", [P, 512])
    pb_sm = pst("pb_sm", [P, 512])
    pb_acc = [pb_a0, pb_a1]
    pb_bc = [pb_b0, pb_b1]

    b = Builder()
    for s_ in ("Tm", "Vm", "Sm", "WG0", "WG1", "WU0", "WU1", "WD0", "WD1",
               "WD2", "XT", "BR", "BW", "OUT", "PR", "PRG", "RS", "XNP"):
        b.newsem(s_)

    ms = b.ms


    def mm(out_ap, lhs_ap, rhs_ap, start, stop, waits=(), inc=None):
        return b.op("t", lambda t, o=out_ap, l=lhs_ap, r=rhs_ap, s1=start,
                    s2=stop: t.matmul(o, l, r, start=s1, stop=s2,
                                      skip_group_check=True),
                    waits, inc)

    def startup():
        loads = [(A_sb, alog_d), (bdt_sb, bdt_d), (dsk_sb, dsk_d),
                 (rwT, rwT_d), (ident, id_d)]
        for dst, src in loads:
            b.op("y", lambda y, d=dst, s=src: y.dma_start(d, s), inc="PR", inc_n=16)
        b.op("g", lambda g: g.dma_start(wdtT, wdtT_d), inc="PRG", inc_n=16)
        b.op("g", lambda g: g.dma_start(wxT, wxT_d), inc="PRG", inc_n=16)
        pr = b.c("PR")
        prg = b.c("PRG")
        b.op("s", lambda s: s.activation(A_sb, A_sb, AF.Exp),
             [("PR", pr), ("PRG", prg)], "Sm")
        sA = b.c("Sm")
        b.op("v", lambda v: v.tensor_scalar_mul(A_sb, A_sb, -1.0),
             [("Sm", sA), ("PR", pr)], "Vm")
        b.op("v", lambda v: v.memset(carry, 0.0), [], "Vm")
        b.op("v", lambda v: v.memset(cst[:, 0:1], EPS), [], "Vm")
        b.op("v", lambda v: v.memset(cst[:, 1:2], 1.0), [], "Vm")
        b.op("v", lambda v: v.memset(cst[:, 2:3], -0.5), [], "Vm")
        ms["startup_v"] = b.c("Vm")
        ms["params"] = pr

    def p1(chunk_tag, xsrc, router):
        guardT = b.c("Tm")
        for tt in range(TT):
            b.op("y", lambda y, a=xsrc(tt): y.dma_start(x_tile, a),
                 [("Vm", ms.get("x_tile_free", 0)), ("Tm", ms.get("x_tile_freeT", 0))],
                 "XT", 16)
            xt = b.c("XT")
            b.op("v", lambda v: [v.bn_stats(st24[:, i * 6:(i + 1) * 6],
                                            x_tile[:, i * 512:(i + 1) * 512])
                                 for i in range(4)][-1], [("XT", xt)], "Vm")
            b.op("v", lambda v: v.bn_aggr(mv, st24), [], "Vm")
            b.op("v", lambda v: v.tensor_scalar_mul(nmean, mv[:, 0:1], -1.0),
                 [], "Vm")
            svm = b.c("Vm")
            b.op("s", lambda s: s.activation(rstd[:, 0:1], mv[:, 1:2], AF.Ln,
                                             bias=cst[:, 0:1]),
                 [("Vm", svm), ("Vm", ms["startup_v"])], "Sm")
            b.op("s", lambda s: s.activation(rstd[:, 1:2], rstd[:, 0:1], AF.Exp,
                                             scale=cst[:, 2:3]), [], "Sm")
            srs = b.c("Sm")
            if p1_level < 2:
                continue
            b.op("v", lambda v: v.tensor_scalar_add(x_tile, x_tile,
                                                    nmean[:, 0:1]), [], "Vm")
            b.op("v", lambda v: v.tensor_scalar_mul(x_tile, x_tile,
                                                    rstd[:, 1:2]),
                 [("Sm", srs)], "Vm")
            vnorm = b.c("Vm")
            if p1_level < 3:
                continue
            for dc in range(DT):
                sl = pb_bc[dc % 2][:, 0:128]
                w_ = [("Vm", ms.get(f"bcV{dc % 2}", 0)),
                      ("Sm", ms.get(f"bcS{dc % 2}", 0))]
                b.op("t", lambda t, o=sl, i_=x_tile[:, dc * 128:(dc + 1) * 128]:
                     t.transpose(o, i_, ident), w_ + [("Vm", vnorm)], "Tm")
                tp = b.c("Tm")
                b.op("s", lambda s, s_=sl, d=xf32[dc % 2]: s.copy(d, s_),
                     [("Tm", tp), ("Vm", ms.get(f"xf32V{dc % 2}", 0))], "Sm")
                ms[f"bcS{dc % 2}"] = b.c("Sm")
                sf = b.c("Sm")
                b.op("v", lambda v, s_=xf32[dc % 2],
                     d=xnT[:, dc * C + tt * 128: dc * C + tt * 128 + 128]:
                     v.tensor_copy(d, s_),
                     [("Sm", sf), ("Tm", guardT), ("XNP", b.c("XNP"))], "Vm")
                ms[f"xf32V{dc % 2}"] = b.c("Vm")
                ms[f"bcV{dc % 2}"] = b.c("Vm")
                if p1_level < 5:
                    continue
                mm(pb_sm[0:96, 0:128], wxT[:, dc * 96:(dc + 1) * 96],
                   xnT[:, dc * C + tt * 128: dc * C + tt * 128 + 128],
                   dc == 0, dc == DT - 1,
                   [("Vm", b.c("Vm")), ("Sm", ms.get("xp_evac", 0)),
                    ("Sm", ms.get("wT_evac", 0)),
                    ("PRG", b.c("PRG"))],
                   "Tm" if dc == DT - 1 else None)
                if router:
                    mm(pb_dl[:, 0:32], xf32[dc % 2],
                       rwT[:, dc * 32:(dc + 1) * 32], dc == 0, dc == DT - 1,
                       [("Sm", sf), ("Sm", ms.get("pbdl_S", 0)),
                        ("Vm", ms.get("pbdl_V", 0))],
                       "Tm" if dc == DT - 1 else None)
            tmm = b.c("Tm")
            ms["x_tile_freeT"] = tmm
            ms["x_tile_free"] = b.c("Vm")
            if p1_level < 6:
                continue
            b.op("s", lambda s, d=xpT[:, tt * 128:(tt + 1) * 128]:
                 s.copy(d, pb_sm[0:96, 0:128]),
                 [("Tm", tmm), ("BR", b.c("BR")), ("RS", b.c("RS")),
                  ("Vm", ms.get("xpT_vfree", 0))], "Sm")
            ms["xp_evac"] = b.c("Sm")
            if router:
                lg = pb_dl[:, 0:32]
                b.op("v", lambda v: v.reduce_max(negm, lg, axis=AX, negate=True),
                     [("Tm", tmm)], "Vm")
                vv = b.c("Vm")
                ms["pbdl_V"] = vv
                b.op("s", lambda s: s.activation(q_sb, lg, AF.Exp,
                                                 bias=negm[:, 0:1]),
                     [("Vm", vv)], "Sm")
                ms["pbdl_S"] = b.c("Sm")
                sq = b.c("Sm")
                b.op("v", lambda v: v.tensor_copy(q_cur, q_sb), [("Sm", sq)], "Vm")
                for k in range(8):
                    b.op("v", lambda v: v.reduce_max(mk, q_cur, axis=AX), [], "Vm")
                    if k == 0:
                        b.op("v", lambda v: v.tensor_copy(accs, mk), [], "Vm")
                    else:
                        b.op("v", lambda v: v.tensor_add(accs, accs, mk), [], "Vm")
                    b.op("v", lambda v: v.tensor_single_scalar(msk, q_cur,
                                                               mk[:, 0:1],
                                                               ALU.is_ge), [], "Vm")
                    b.op("v", lambda v: v.scalar_tensor_tensor(q_cur, msk, -2.0,
                                                               q_cur, ALU.mult,
                                                               ALU.add), [], "Vm")
                b.op("v", lambda v: v.reciprocal(rcp, accs), [], "Vm")
                b.op("v", lambda v: v.tensor_sub(selw, q_sb, q_cur), [], "Vm")
                b.op("v", lambda v: v.tensor_scalar(selw, selw, 0.5, None,
                                                    ALU.mult), [], "Vm")
                b.op("v", lambda v: v.tensor_scalar_mul(w_t, q_sb, rcp[:, 0:1]),
                     [], "Vm")
                b.op("v", lambda v: v.tensor_mul(w_t, w_t, selw), [], "Vm")
                vw = b.c("Vm")
                b.op("t", lambda t: t.transpose(pb_sm[0:32, 256:384], w_t,
                                                ident),
                     [("Vm", vw), ("Sm", ms.get("wT_evac", 0))], "Tm")
                twt = b.c("Tm")
                b.op("s", lambda s: s.copy(wT, pb_sm[0:32, 256:384]),
                     [("Tm", twt), ("RS", ms.get("wT_rs", 0))], "Sm")
                ms["wT_evac"] = b.c("Sm")
                swt = b.c("Sm")
                b.op("y", lambda y, d=rows_d[32:64, tt * 128:(tt + 1) * 128]:
                     y.dma_start(d, wT), [("Sm", swt)], "RS", 16)
                ms["wT_rs"] = b.c("RS")
        ms[f"xnT_done_{chunk_tag}"] = b.c("Vm")
        ms[f"xp_done_{chunk_tag}"] = b.c("Sm")
        if p1_level < 7:
            return
        sx = b.c("Sm")
        if router:
            b.op("y", lambda y: y.dma_start(rows_d[0:32, :], xpT[64:96, :]),
                 [("Sm", sx), ("BR", b.c("BR")), ("BW", b.c("BW"))], "RS", 16)
        else:
            sub = int(chunk_tag[1])
            b.op("v", lambda v, d=xpT_p[sub]: v.tensor_copy(d, xpT),
                 [("Sm", sx)], "Vm")
            ms[f"xpp_ready_{chunk_tag}"] = b.c("Vm")
            ms["xpT_vfree"] = b.c("Vm")
            b.op("y", lambda y, r0=64 + 16 * sub:
                 y.dma_start(rows_d[r0:r0 + 16, :], xpT[64:80, :]),
                 [("Sm", sx), ("BR", b.c("BR"))], "RS", 16)
            vx = b.c("Vm")
            b.op("y", lambda y, d=xnp_d[sub]: y.dma_start(d, xnT),
                 [("Vm", vx)], "XNP", 16)
        ms[f"rows_{chunk_tag}"] = b.c("RS")

    def mamba_prologue(chunk_tag, dt):
        if chunk_tag[0] == "p":
            xp_src = xpT_p[int(chunk_tag[1])]
        else:
            xp_src = xpT
        for hf in range(2):
            mm(pb_dl, wdtT[:, dt * 128:(dt + 1) * 128],
               xp_src[0:64, hf * 512:(hf + 1) * 512], True, True,
               [("Sm", ms.get("pbdl_S", 0)), ("Vm", ms.get("pbdl_V", 0)),
                ("PRG", b.c("PRG")),
                ("Sm", ms.get(f"xp_done_{chunk_tag}", 0)),
                ("Vm", ms.get(f"xpp_ready_{chunk_tag}", 0))], "Tm")
            tdl = b.c("Tm")
            b.op("s", lambda s, bia=bdt_sb[:, dt:dt + 1]:
                 s.activation(dscr, pb_dl, AF.Exp, bias=bia),
                 [("Tm", tdl), ("Vm", ms.get("dl_war", 0)),
                  ("OUT", ms.get("out_dma", 0)),
                  ("Vm", ms.get("out_tile_vfree", 0))], "Sm")
            ms["pbdl_S"] = b.c("Sm")
            b.op("s", lambda s, d=delta[:, hf * 512:(hf + 1) * 512]:
                 s.activation(d, dscr, AF.Ln, bias=cst[:, 1:2]), [], "Sm")
        sdl = b.c("Sm")
        if chunk_tag[0] == "p":
            sub = int(chunk_tag[1])
            b.op("y", lambda y, s_=xnp_d[sub][:, dt * C:(dt + 1) * C]:
                 y.dma_start(dxt, s_),
                 [("Vm", b.c("Vm")), ("XNP", 16 * (sub + 1))], "XT", 16)
            xl = b.c("XT")
            b.op("v", lambda v: v.tensor_mul(dxt, delta, dxt),
                 [("Sm", sdl), ("XT", xl)], "Vm")
        else:
            b.op("v", lambda v, xs=xnT[:, dt * C:(dt + 1) * C]:
                 v.tensor_mul(dxt, delta, xs), [("Sm", sdl)], "Vm")
        ms["dl_war"] = b.c("Vm")

    def mamba_nstep(chunk_tag, dt, n, do_y):
        if True:
            ci = dt * NST + n
            rb = 0 if chunk_tag[0] == "c" else 64 + 16 * int(chunk_tag[1])
            sl_ = n % 2
            b.op("y", lambda y, n_=rb + n, s2=sl_:
                 y.dma_start(B_sb[:, s2 * C:(s2 + 1) * C],
                             rows_d[n_:n_ + 1, :].broadcast_to([P, C])),
                 [("Vm", ms.get(f"Bw{sl_}", 0)),
                  ("RS", ms.get(f"rows_{chunk_tag}", 0))], "BR", 16)
            br = b.c("BR")
            if do_y:
                b.op("y", lambda y, n_=16 + n, s2=sl_:
                     y.dma_start(C_sb[:, s2 * C:(s2 + 1) * C],
                                 rows_d[n_:n_ + 1, :].broadcast_to([P, C])),
                     [("Vm", ms.get(f"Cw{sl_}", 0)),
                      ("RS", ms.get(f"rows_{chunk_tag}", 0))], "BW", 16)
            brc = b.c("BW")
            b.op("s", lambda s, i=ci, s2=sl_: s.activation(
                abar[:, s2 * C:(s2 + 1) * C], delta,
                AF.Exp, scale=A_sb[:, i:i + 1]),
                 [("Vm", ms.get(f"abar_war{sl_}", 0))], "Sm")
            sA = b.c("Sm")
            b.op("v", lambda v, s2=sl_: v.tensor_mul(
                bx, dxt, B_sb[:, s2 * C:(s2 + 1) * C]), [("BR", br)], "Vm")
            ms[f"Bw{sl_}"] = b.c("Vm")
            b.op("v", lambda v, i=ci, s2=sl_: v.tensor_tensor_scan(
                hsc, abar[:, s2 * C:(s2 + 1) * C], bx, carry[:, i:i + 1],
                ALU.mult, ALU.add), [("Sm", sA)], "Vm")
            ms[f"abar_war{sl_}"] = b.c("Vm")
            b.op("v", lambda v, i=ci: v.tensor_copy(carry[:, i:i + 1],
                                                    hsc[:, C - 1:C]), [], "Vm")
            if do_y:
                if n == 0:
                    b.op("v", lambda v, s2=sl_: v.tensor_mul(
                        ybuf, hsc, C_sb[:, s2 * C:(s2 + 1) * C]),
                        [("BW", brc)], "Vm")
                else:
                    b.op("v", lambda v, s2=sl_: v.tensor_mul(
                        bx, hsc, C_sb[:, s2 * C:(s2 + 1) * C]),
                        [("BW", brc)], "Vm")
                    b.op("v", lambda v: v.tensor_add(ybuf, ybuf, bx), [], "Vm")
                ms[f"Cw{sl_}"] = b.c("Vm")

    def mamba_epilogue(dt):
        b.op("v", lambda v, xs=xnT[:, dt * C:(dt + 1) * C],
             dc_=dsk_sb[:, dt:dt + 1]: v.scalar_tensor_tensor(
            ybuf, xs, dc_, ybuf, ALU.mult, ALU.add), [], "Vm")
        b.op("v", lambda v, d=acc[:, dt * C:(dt + 1) * C]:
             v.tensor_add(d, d, ybuf), [], "Vm")
        ms["mamba_v"] = b.c("Vm")

    def mamba_units(chunk_tag, dt, do_y):
        u = [lambda: mamba_prologue(chunk_tag, dt)]
        for n in range(NST):
            u.append(lambda n=n: mamba_nstep(chunk_tag, dt, n, do_y))
        if do_y:
            u.append(lambda: mamba_epilogue(dt))
        return u

    def mamba_dt(chunk_tag, dt, do_y):
        for u in mamba_units(chunk_tag, dt, do_y):
            u()

    wload_idx = {"wg": 0, "wu": 0, "wd": 0}

    def load_gu_col(nm, dr, slots, e, ic):
        k = wload_idx[nm]
        slot = k % 2
        wload_idx[nm] = k + 1
        sem = ("WG" if nm == "wg" else "WU") + str(slot)
        freekey = f"free_{nm}{slot}"
        b.op("g", lambda g, s_=slots[slot], a=dr[e, ic]: g.dma_start(s_, a),
             [("Tm", ms.get(freekey, 0))], sem, 16)
        return slot, b.c(sem)

    def load_wd_col(e, hc):
        k = wload_idx["wd"]
        slot = k % 3
        wload_idx["wd"] = k + 1
        sem = f"WD{slot}"
        b.op("g", lambda g, s_=wds[slot],
             a=wd_d[e, hc // 2, :, (hc % 2) * 768:(hc % 2) * 768 + 768]:
             g.dma_start(s_, a),
             [("Tm", ms.get(f"free_wd{slot}", 0))], sem, 16)
        return slot, b.c(sem)

    def pair(chunk_tag, p_idx, extra=None):
        extra = list(extra or [])
        n_slots = 2 * ICT + 16
        ne = len(extra)
        drip_sched = []
        for i in range(n_slots):
            if i < 2 * ICT:
                drip_sched.append((ne * 3 * (i + 1)) // (10 * 2 * ICT))
            else:
                base = (ne * 3) // 10
                drip_sched.append(base + ((ne - base) * (i - 2 * ICT + 1))
                                  // 16)
        slot_i = [0]

        def drip():
            want = drip_sched[min(slot_i[0], n_slots - 1)]
            slot_i[0] += 1
            while extra and (len(drip_sched) - len(extra) >= 0):
                done = drip_sched[-1] - len(extra)
                if done < want:
                    extra.pop(0)()
                else:
                    break

        e0, e1 = 2 * p_idx, 2 * p_idx + 1
        xn_ok = ms.get(f"xnT_done_{chunk_tag}", 0)
        for ei, e in enumerate((e0, e1)):
            b.op("y", lambda y, e_=e, i=ei: y.dma_start(
                wb[:, i * C:(i + 1) * C],
                rows_d[32 + e_:33 + e_, :].broadcast_to([P, C])),
                [("Vm", ms.get("wb_war", 0)),
                 ("RS", ms.get(f"rows_{chunk_tag}", 0))], "BW", 16)
        swb = b.c("BW")
        for ei, e in enumerate((e0, e1)):
            for ic in range(ICT):
                gslot, gsem = load_gu_col("wg", wg_d, wgs, e, ic)
                uslot, usem = load_gu_col("wu", wu_d, wus, e, ic)
                for sl in range(2):
                    j2 = (ic * 2 + sl) % 2
                    pg = [pb_g, pb_b0][0]
                    pu = [pb_u, pb_b1][0]
                    for hc in range(16):
                        w_ = [(f"WG{gslot}", gsem), ("Vm", xn_ok)]
                        if hc == 0:
                            w_.append(("Sm", ms.get("g_evac0", 0)))
                        mm(pg[:], wgs[gslot][:, hc * 128:(hc + 1) * 128],
                           xnT[:, hc * C + sl * 512: hc * C + (sl + 1) * 512],
                           hc == 0, hc == 15, w_, "Tm" if hc == 15 else None)
                    tg = b.c("Tm")
                    if sl == 1:
                        ms[f"free_wg{gslot}"] = tg
                    for hc in range(16):
                        w_ = [(f"WU{uslot}", usem)]
                        if hc == 0:
                            w_.append(("Sm", ms.get("u_evac0", 0)))
                        mm(pu[:], wus[uslot][:, hc * 128:(hc + 1) * 128],
                           xnT[:, hc * C + sl * 512: hc * C + (sl + 1) * 512],
                           hc == 0, hc == 15, w_, "Tm" if hc == 15 else None)
                    tu = b.c("Tm")
                    if sl == 1:
                        ms[f"free_wu{uslot}"] = tu
                    b.op("s", lambda s, j=j2, p_=pg: s.copy(
                        g_sb[:, j * 512:(j + 1) * 512], p_[:]),
                        [("Tm", tg), ("Sm", ms.get(f"silu{j2}", 0))], "Sm")
                    ms["g_evac0"] = b.c("Sm")
                    b.op("s", lambda s, j=j2, p_=pu: s.copy(
                        u_sb[:, j * 512:(j + 1) * 512], p_[:]),
                        [("Tm", tu), ("Vm", ms.get(f"h1_{j2}", 0))], "Sm")
                    ms["u_evac0"] = b.c("Sm")
                    b.op("s", lambda s, j=j2: s.activation(
                        sg_sb[:, 0:512],
                        g_sb[:, j * 512:(j + 1) * 512], AF.Silu),
                        [("Vm", ms.get("sg_war", 0))], "Sm")
                    ms[f"silu{j2}"] = b.c("Sm")
                    ssl = b.c("Sm")
                    b.op("v", lambda v, j=j2: v.tensor_mul(
                        sg_sb[:, 0:512],
                        sg_sb[:, 0:512],
                        u_sb[:, j * 512:(j + 1) * 512]), [("Sm", ssl)], "Vm")
                    ms[f"h1_{j2}"] = b.c("Vm")
                    b.op("v", lambda v, j=j2, i=ei, c_=ic, s_=sl: v.tensor_mul(
                        harena[:, i * ICT * C + c_ * C + s_ * 512:
                               i * ICT * C + c_ * C + s_ * 512 + 512],
                        sg_sb[:, 0:512],
                        wb[:, i * C + s_ * 512: i * C + s_ * 512 + 512]),
                        [("BW", swb), ("Tm", ms.get("h_rdT", 0))], "Vm")
                    ms["sg_war"] = b.c("Vm")
                drip()
            ms[f"h_done{ei}"] = b.c("Vm")
        ms["wb_war"] = b.c("Vm")
        hdone = b.c("Vm")
        for hc in range(16):
            s0, d0 = load_wd_col(e0, hc)
            s1, d1 = load_wd_col(e1, hc)
            for sl in range(2):
                k = (hc * 2 + sl) % 2
                nmm = 0
                for ei, (e, ws_, dsem) in enumerate(
                        ((e0, s0, d0), (e1, s1, d1))):
                    for icc in range(ICT):
                        w_ = [(f"WD{ws_}", dsem), ("Vm", hdone)]
                        if nmm == 0:
                            w_.append(("Vm", ms.get(f"acc_evac{k}", 0)))
                        mm(pb_acc[k][:],
                           wds[ws_][:, icc * 128:(icc + 1) * 128],
                           harena[:, ei * ICT * C + icc * C + sl * 512:
                                  ei * ICT * C + icc * C + sl * 512 + 512],
                           nmm == 0, nmm == 11, w_,
                           "Tm" if nmm == 11 else None)
                        nmm += 1
                tdn = b.c("Tm")
                if sl == 1:
                    ms[f"free_wd{s0}"] = tdn
                    ms[f"free_wd{s1}"] = tdn
                b.op("v", lambda v, h_=hc, s_=sl, k_=k: v.tensor_add(
                    acc[:, h_ * C + s_ * 512: h_ * C + s_ * 512 + 512],
                    acc[:, h_ * C + s_ * 512: h_ * C + s_ * 512 + 512],
                    pb_acc[k_][:]), [("Tm", tdn)], "Vm")
                ms[f"acc_evac{k}"] = b.c("Vm")
            drip()
        while extra:
            extra.pop(0)()
        ms["h_rdT"] = b.c("Tm")

    def out_assembly(c):
        vacc = b.c("Vm")
        s_dscr = b.c("Sm")
        for tt in range(TT):
            b.op("y", lambda y, t_=tt: y.dma_start(
                x_tile, xm_d[c * C + t_ * 128: c * C + (t_ + 1) * 128, :]),
                [("Vm", ms.get("x_tile_free", 0)),
                 ("Tm", ms.get("x_tile_freeT", 0))], "XT", 16)
            xt = b.c("XT")
            for half in range(2):
                for d8 in range(8):
                    dc = half * 8 + d8
                    sl = pb_bc[d8 % 2][:, 0:128]
                    b.op("t", lambda t, o=sl,
                         i_=acc[:, dc * C + tt * 128: dc * C + tt * 128 + 128]:
                         t.transpose(o, i_, ident),
                         [("Vm", vacc), ("Vm", ms.get(f"bcV{d8 % 2}", 0)),
                          ("Sm", ms.get(f"bcS{d8 % 2}", 0))], "Tm")
                    ttp = b.c("Tm")
                    b.op("v", lambda v, s_=sl, d8_=d8, dc_=dc: v.tensor_add(
                        out_tile[:, d8_ * 128:(d8_ + 1) * 128], s_,
                        x_tile[:, dc_ * 128:(dc_ + 1) * 128]),
                        [("Tm", ttp), ("XT", xt), ("Sm", s_dscr),
                         ("OUT", ms.get("out_dma", 0))], "Vm")
                    ms[f"bcV{d8 % 2}"] = b.c("Vm")
                vout = b.c("Vm")
                b.op("y", lambda y, t_=tt, h_=half: y.dma_start(
                    out_d[c * C + t_ * 128: c * C + (t_ + 1) * 128,
                          h_ * 1024:(h_ + 1) * 1024], out_tile),
                    [("Vm", vout)], "OUT", 16)
                ms["out_dma"] = b.c("OUT")
            ms["x_tile_free"] = b.c("Vm")
            ms["x_tile_freeT"] = b.c("Tm")
            ms["out_tile_vfree"] = b.c("Vm")

    startup()
    if do_p0:
        for sub in range(2):
            p1(f"p{sub}", lambda tt, s_=sub: xp_d[s_ * C + tt * 128:
                                                  s_ * C + (tt + 1) * 128, :],
               False)
    for c in range(2):
        tag = f"c{c}"
        b.op("v", lambda v: v.memset(acc, 0.0),
             [("Tm", ms.get("x_tile_freeT", 0))], "Vm")
        if do_p1:
            p1(tag, lambda tt, c_=c: xm_d[c_ * C + tt * 128:
                                          c_ * C + (tt + 1) * 128, :], do_router)
        if c == 0 and do_p0 and do_mamba and os.environ.get("KP0W") != "1":
            for dt in range(DT):
                mamba_dt("p0", dt, False)
                mamba_dt("p1", dt, False)
        for p_idx in range(16):
            extra = []
            if c == 0 and do_p0 and do_mamba and os.environ.get("KP0W") == "1":
                extra += mamba_units("p0", p_idx, False)
                extra += mamba_units("p1", p_idx, False)
            if do_mamba and os.environ.get("KP0ONLY") != "1":
                extra += mamba_units(tag, p_idx, True)
            if do_ffn:
                pair(tag, p_idx, extra)
            else:
                for u in extra:
                    u()
        if do_out:
            out_assembly(c)
    if _dbg:
        b.op("y", lambda y: y.dma_start(dbgc_d, carry),
             [("Vm", b.c("Vm"))], "OUT", 16)
    b.op("y", lambda y: y.nop(), [("OUT", b.c("OUT"))])

    sems = {}
    for s_ in b.cnt:
        sems[s_] = es.enter_context(nc.semaphore(s_))

    def emit(section, eng_key):
        waited = {}
        for (waits, fn, inc, inc_n) in b.ev[eng_key]:
            for sn, v in waits:
                if v > waited.get(sn, 0):
                    section.wait_ge(sems[sn], v)
                    waited[sn] = v
            inst = fn(section)
            if inc is not None:
                inst.then_inc(sems[inc], inc_n)

    with es, nc.Block() as block:
        @block.sync
        def _(sync):
            emit(sync, "y")

        @block.gpsimd
        def _(gpsimd):
            emit(gpsimd, "g")

        @block.tensor
        def _(tensor):
            emit(tensor, "t")

        @block.scalar
        def _(scalar):
            emit(scalar, "s")

        @block.vector
        def _(vector):
            emit(vector, "v")

    return nc


_NC_CACHE = None


def _get_nc():
    global _NC_CACHE
    if _NC_CACHE is None:
        _NC_CACHE = build_nc()
    return _NC_CACHE


def kernel(L0_attn_x, L0_attn_A_log, L0_attn_D, L0_attn_W_x, L0_attn_W_dt,
           L0_attn_b_dt, L0_ffn_router_w, L0_ffn_w_gate, L0_ffn_w_up,
           L0_ffn_w_down):
    x = np.ascontiguousarray(np.asarray(L0_attn_x, dtype=np.float32))
    A_log = np.asarray(L0_attn_A_log, np.float32)
    Dsk = np.asarray(L0_attn_D, np.float32)
    W_x = np.asarray(L0_attn_W_x, np.float32)
    W_dt = np.asarray(L0_attn_W_dt, np.float32)
    b_dt = np.asarray(L0_attn_b_dt, np.float32)
    rw = np.asarray(L0_ffn_router_w, np.float32)
    wg = np.asarray(L0_ffn_w_gate, np.float32)
    wu = np.asarray(L0_ffn_w_up, np.float32)
    wd = np.asarray(L0_ffn_w_down, np.float32)

    wg_r = np.ascontiguousarray(
        wg.reshape(E, 16, 128, ICT, 128).transpose(0, 3, 2, 1, 4)
    ).reshape(E, ICT, 128, 2048)
    wu_r = np.ascontiguousarray(
        wu.reshape(E, 16, 128, ICT, 128).transpose(0, 3, 2, 1, 4)
    ).reshape(E, ICT, 128, 2048)
    wd_r = np.ascontiguousarray(
        wd.reshape(E, ICT, 128, 16, 128).transpose(0, 3, 2, 1, 4)
    )
    wd_r = np.ascontiguousarray(
        wd_r.reshape(E, 8, 2, 128, 768).transpose(0, 1, 3, 2, 4)
    ).reshape(E, 8, 128, 1536)
    wxT = np.ascontiguousarray(
        W_x.T.reshape(16, 128, 96).transpose(1, 0, 2)).reshape(128, 16 * 96)
    rwT = np.ascontiguousarray(
        rw.T.reshape(16, 128, 32).transpose(1, 0, 2)).reshape(128, 16 * 32)
    wdtT = np.ascontiguousarray(W_dt.T)
    alog = np.ascontiguousarray(
        A_log.reshape(16, 128, 16).transpose(1, 0, 2)).reshape(128, 256)
    bdt = np.ascontiguousarray(b_dt.reshape(16, 128).T)
    dsk = np.ascontiguousarray(Dsk.reshape(16, 128).T)
    ident = np.eye(128, dtype=np.float32)
    onb = np.ones((1, 128), dtype=ml_dtypes.bfloat16)

    shared = dict(wg=wg_r, wu=wu_r, wd=wd_r, wxT=wxT, rwT=rwT, wdtT=wdtT,
                  alog=alog, bdt=bdt, dsk=dsk, idn=ident, onb=onb)
    zeros = np.zeros((2048, 2048), np.float32)
    in_maps = []
    for core in range(8):
        bb, half = core // 2, core % 2
        xm = np.ascontiguousarray(x[bb, half * 2048:(half + 1) * 2048])
        xq = zeros if half == 0 else np.ascontiguousarray(x[bb, 0:2048])
        in_maps.append(dict(xm=xm, xq=xq, **shared))

    nc = _get_nc()
    res = run_bass_kernel_spmd(nc, in_maps, core_ids=list(range(8)))
    outs = [res.results[i]["out"] for i in range(8)]
    full = np.concatenate(outs, axis=0).reshape(4, 4096, 2048)
    return full


# version 12
# speedup vs baseline: 1.0627x; 1.0270x over previous
import numpy as np
import ml_dtypes
import contextlib
import os
import concourse.bass as bass
import concourse.mybir as mybir
import concourse.bass_utils as _bu
from concourse.bass_utils import run_bass_kernel_spmd

if os.environ.get("KLDW", "1") == "1" and not getattr(_bu, "_ldw_patched", False):
    _orig_run_command = _bu.run_command

    def _run_command_ldw(argv, **kwargs):
        argv = ["--enable-ldw-opt=true" if a == "--enable-ldw-opt=false" else a
                for a in argv]
        return _orig_run_command(argv, **kwargs)

    _bu.run_command = _run_command_ldw
    _bu._ldw_patched = True

F32 = mybir.dt.float32
BF16 = mybir.dt.bfloat16
AF = mybir.ActivationFunctionType
ALU = mybir.AluOpType
AX = mybir.AxisListType.X

P = 128
C = 1024
TT = 8
DT = 16
NST = 16
ICT = 6
E = 32
EPS = 1e-5

ENGINES = ("t", "s", "v", "y", "g")


class Builder:
    def __init__(self):
        self.ev = {e: [] for e in ENGINES}
        self.cnt = {}
        self.ms = {}

    def newsem(self, name):
        self.cnt[name] = 0

    CHAIN = {"v": "Vm", "s": "Sm"}

    def op(self, eng, fn, waits=(), inc=None, inc_n=1):
        w = [(sn, v) for (sn, v) in waits if v > 0]
        ch = self.CHAIN.get(eng)
        if ch is not None:
            if self.cnt[ch] > 0:
                w.append((ch, self.cnt[ch]))
            if inc is None:
                inc, inc_n = ch, 1
            else:
                assert inc == ch
        if inc is not None:
            if inc_n == 16 and self.cnt[inc] > 0:
                w.append((inc, self.cnt[inc]))
            self.cnt[inc] += inc_n
        self.ev[eng].append((tuple(w), fn, inc, inc_n))
        return self.cnt[inc] if inc is not None else None

    def c(self, name):
        return self.cnt[name]


def build_nc(do_p0=True, do_mamba=True, do_ffn=True, do_p1=True, do_router=True, do_out=True, p1_level=9):
    nc = bass.Bass("TRN2", target_bir_lowering=False, debug=False)

    xm_d = nc.dram_tensor("xm", [2048, 2048], F32, kind="ExternalInput").ap()
    xp_d = nc.dram_tensor("xq", [2048, 2048], F32, kind="ExternalInput").ap()
    wg_d = nc.dram_tensor("wg", [E, ICT, 128, 2048], F32, kind="ExternalInput").ap()
    wu_d = nc.dram_tensor("wu", [E, ICT, 128, 2048], F32, kind="ExternalInput").ap()
    wd_d = nc.dram_tensor("wd", [E, 8, 128, 1536], F32, kind="ExternalInput").ap()
    wxT_d = nc.dram_tensor("wxT", [128, 16 * 96], F32, kind="ExternalInput").ap()
    rwT_d = nc.dram_tensor("rwT", [128, 16 * 32], F32, kind="ExternalInput").ap()
    wdtT_d = nc.dram_tensor("wdtT", [64, 2048], F32, kind="ExternalInput").ap()
    alog_d = nc.dram_tensor("alog", [128, 256], F32, kind="ExternalInput").ap()
    bdt_d = nc.dram_tensor("bdt", [128, 16], F32, kind="ExternalInput").ap()
    dsk_d = nc.dram_tensor("dsk", [128, 16], F32, kind="ExternalInput").ap()
    id_d = nc.dram_tensor("idn", [128, 128], F32, kind="ExternalInput").ap()
    onb_d = nc.dram_tensor("onb", [1, 128], BF16, kind="ExternalInput").ap()
    out_d = nc.dram_tensor("out", [2048, 2048], F32, kind="ExternalOutput").ap()
    _dbg = os.environ.get("KDBG_CARRY") == "1"
    dbgc_d = (nc.dram_tensor("dbg_carry", [128, DT * NST], F32,
                             kind="ExternalOutput").ap() if _dbg else None)
    rows_d = nc.dram_tensor("rows_scr", [96, 1024], BF16).ap()
    xnp_d = [nc.dram_tensor(f"xnp{i}", [128, DT * C], BF16).ap() for i in range(2)]

    es = contextlib.ExitStack()
    sb = lambda n_, s, d: es.enter_context(nc.sbuf_tensor(n_, s, d))[:]
    pst = lambda n_, s: es.enter_context(nc.psum_tensor(n_, s, F32))[:]

    acc = sb("acc", [P, DT * C], F32)
    xnT = sb("xnT", [P, DT * C], BF16)
    wgs = [sb(f"wgs{i}", [P, 16 * 128], BF16) for i in range(2)]
    wus = [sb(f"wus{i}", [P, 16 * 128], BF16) for i in range(2)]
    wds = [sb(f"wds{i}", [P, 768], BF16) for i in range(3)]
    harena = sb("harena", [P, 2 * ICT * C], BF16)
    wb = sb("wb", [P, 2 * C], BF16)
    g_sb = sb("g_sb", [P, 1024], BF16)
    u_sb = sb("u_sb", [P, 1024], BF16)
    sg_sb = sb("sg_sb", [P, 512], BF16)
    B_sb = sb("B_sb", [P, 2 * C], BF16)
    C_sb = sb("C_sb", [P, 2 * C], BF16)
    abar = sb("abar", [P, 2 * C], F32)
    bx = sb("bx", [P, C], BF16)
    hsc = sb("hsc", [P, C], BF16)
    ybuf = sb("ybuf", [P, C], F32)
    dxt = sb("dxt", [P, C], BF16)
    delta = sb("delta", [P, C], BF16)
    x_tile = sb("x_tile", [P, 2048], F32)
    out_tile = sb("out_tile", [P, C], F32)
    dscr = out_tile[:, 0:512]
    xf32 = [sb(f"xf32_{i}", [P, 128], F32) for i in range(2)]
    xpT = sb("xpT", [96, C], BF16)
    xpT_p = [out_tile[:, 512:1024].bitcast(BF16)[0:96, :],
             sb("xpT_p1", [96, C], BF16)[0:96, :]]
    carry = sb("carry", [P, DT * NST], F32)
    A_sb = sb("A_sb", [P, DT * NST], F32)
    bdt_sb = sb("bdt_sb", [P, 16], F32)
    dsk_sb = sb("dsk_sb", [P, 16], F32)
    wxT = sb("wxT_s", [P, 16 * 96], BF16)
    rwT = sb("rwT_s", [P, 16 * 32], F32)
    wdtT = sb("wdtT_s", [64, 2048], BF16)
    wT = sb("wT", [32, 128], BF16)
    ident = sb("ident", [P, 128], F32)
    st24 = sb("st24", [P, 24], F32)
    mv = sb("mv", [P, 2], F32)
    nmean = sb("nmean", [P, 1], F32)
    rstd = sb("rstd", [P, 2], F32)
    q_sb = sb("q_sb", [P, 32], F32)
    q_cur = sb("q_cur", [P, 32], F32)
    msk = sb("msk", [P, 32], F32)
    selw = sb("selw", [P, 32], F32)
    w_t = sb("w_t", [P, 32], F32)
    negm = sb("negm", [P, 1], F32)
    accs = sb("accs", [P, 1], F32)
    mk = sb("mk", [P, 1], F32)
    rcp = sb("rcp", [P, 1], F32)
    cst = sb("cst", [P, 3], F32)

    pb_g = pst("pb_g", [P, 512])
    pb_u = pst("pb_u", [P, 512])
    pb_a0 = pst("pb_a0", [P, 512])
    pb_a1 = pst("pb_a1", [P, 512])
    pb_b0 = pst("pb_b0", [P, 512])
    pb_b1 = pst("pb_b1", [P, 512])
    pb_dl = pst("pb_dl", [P, 512])
    pb_sm = pst("pb_sm", [P, 512])
    pb_acc = [pb_a0, pb_a1]
    pb_bc = [pb_b0, pb_b1]

    b = Builder()
    for s_ in ("Tm", "Vm", "Sm", "WG0", "WG1", "WU0", "WU1", "WD0", "WD1",
               "WD2", "XT", "BR", "BW", "OUT", "PR", "PRG", "RS", "XNP"):
        b.newsem(s_)

    ms = b.ms


    def mm(out_ap, lhs_ap, rhs_ap, start, stop, waits=(), inc=None):
        return b.op("t", lambda t, o=out_ap, l=lhs_ap, r=rhs_ap, s1=start,
                    s2=stop: t.matmul(o, l, r, start=s1, stop=s2,
                                      skip_group_check=True),
                    waits, inc)

    def startup():
        loads = [(A_sb, alog_d), (bdt_sb, bdt_d), (dsk_sb, dsk_d),
                 (rwT, rwT_d), (ident, id_d)]
        for dst, src in loads:
            b.op("y", lambda y, d=dst, s=src: y.dma_start(d, s), inc="PR", inc_n=16)
        b.op("g", lambda g: g.dma_start(wdtT, wdtT_d), inc="PRG", inc_n=16)
        b.op("g", lambda g: g.dma_start(wxT, wxT_d), inc="PRG", inc_n=16)
        pr = b.c("PR")
        prg = b.c("PRG")
        b.op("s", lambda s: s.activation(A_sb, A_sb, AF.Exp),
             [("PR", pr), ("PRG", prg)], "Sm")
        sA = b.c("Sm")
        b.op("v", lambda v: v.tensor_scalar_mul(A_sb, A_sb, -1.0),
             [("Sm", sA), ("PR", pr)], "Vm")
        b.op("v", lambda v: v.memset(carry, 0.0), [], "Vm")
        b.op("v", lambda v: v.memset(cst[:, 0:1], EPS), [], "Vm")
        b.op("v", lambda v: v.memset(cst[:, 1:2], 1.0), [], "Vm")
        b.op("v", lambda v: v.memset(cst[:, 2:3], -0.5), [], "Vm")
        ms["startup_v"] = b.c("Vm")
        ms["params"] = pr

    def p1(chunk_tag, xsrc, router):
        guardT = b.c("Tm")
        for tt in range(TT):
            b.op("y", lambda y, a=xsrc(tt): y.dma_start(x_tile, a),
                 [("Vm", ms.get("x_tile_free", 0)), ("Tm", ms.get("x_tile_freeT", 0))],
                 "XT", 16)
            xt = b.c("XT")
            b.op("v", lambda v: [v.bn_stats(st24[:, i * 6:(i + 1) * 6],
                                            x_tile[:, i * 512:(i + 1) * 512])
                                 for i in range(4)][-1], [("XT", xt)], "Vm")
            b.op("v", lambda v: v.bn_aggr(mv, st24), [], "Vm")
            b.op("v", lambda v: v.tensor_scalar_mul(nmean, mv[:, 0:1], -1.0),
                 [], "Vm")
            svm = b.c("Vm")
            b.op("s", lambda s: s.activation(rstd[:, 0:1], mv[:, 1:2], AF.Ln,
                                             bias=cst[:, 0:1]),
                 [("Vm", svm), ("Vm", ms["startup_v"])], "Sm")
            b.op("s", lambda s: s.activation(rstd[:, 1:2], rstd[:, 0:1], AF.Exp,
                                             scale=cst[:, 2:3]), [], "Sm")
            srs = b.c("Sm")
            if p1_level < 2:
                continue
            b.op("v", lambda v: v.tensor_scalar_add(x_tile, x_tile,
                                                    nmean[:, 0:1]), [], "Vm")
            b.op("v", lambda v: v.tensor_scalar_mul(x_tile, x_tile,
                                                    rstd[:, 1:2]),
                 [("Sm", srs)], "Vm")
            vnorm = b.c("Vm")
            if p1_level < 3:
                continue
            for dc in range(DT):
                sl = pb_bc[dc % 2][:, 0:128]
                w_ = [("Vm", ms.get(f"bcV{dc % 2}", 0)),
                      ("Sm", ms.get(f"bcS{dc % 2}", 0))]
                b.op("t", lambda t, o=sl, i_=x_tile[:, dc * 128:(dc + 1) * 128]:
                     t.transpose(o, i_, ident), w_ + [("Vm", vnorm)], "Tm")
                tp = b.c("Tm")
                b.op("s", lambda s, s_=sl, d=xf32[dc % 2]: s.copy(d, s_),
                     [("Tm", tp), ("Vm", ms.get(f"xf32V{dc % 2}", 0))], "Sm")
                ms[f"bcS{dc % 2}"] = b.c("Sm")
                sf = b.c("Sm")
                b.op("v", lambda v, s_=xf32[dc % 2],
                     d=xnT[:, dc * C + tt * 128: dc * C + tt * 128 + 128]:
                     v.tensor_copy(d, s_),
                     [("Sm", sf), ("Tm", guardT), ("XNP", b.c("XNP"))], "Vm")
                ms[f"xf32V{dc % 2}"] = b.c("Vm")
                ms[f"bcV{dc % 2}"] = b.c("Vm")
                if p1_level < 5:
                    continue
                mm(pb_sm[0:96, 0:128], wxT[:, dc * 96:(dc + 1) * 96],
                   xnT[:, dc * C + tt * 128: dc * C + tt * 128 + 128],
                   dc == 0, dc == DT - 1,
                   [("Vm", b.c("Vm")), ("Sm", ms.get("xp_evac", 0)),
                    ("Sm", ms.get("wT_evac", 0)),
                    ("PRG", b.c("PRG"))],
                   "Tm" if dc == DT - 1 else None)
                if router:
                    mm(pb_dl[:, 0:32], xf32[dc % 2],
                       rwT[:, dc * 32:(dc + 1) * 32], dc == 0, dc == DT - 1,
                       [("Sm", sf), ("Sm", ms.get("pbdl_S", 0)),
                        ("Vm", ms.get("pbdl_V", 0))],
                       "Tm" if dc == DT - 1 else None)
            tmm = b.c("Tm")
            ms["x_tile_freeT"] = tmm
            ms["x_tile_free"] = b.c("Vm")
            if p1_level < 6:
                continue
            b.op("s", lambda s, d=xpT[:, tt * 128:(tt + 1) * 128]:
                 s.copy(d, pb_sm[0:96, 0:128]),
                 [("Tm", tmm), ("BR", b.c("BR")), ("RS", b.c("RS")),
                  ("Vm", ms.get("xpT_vfree", 0))], "Sm")
            ms["xp_evac"] = b.c("Sm")
            if router:
                lg = pb_dl[:, 0:32]
                b.op("v", lambda v: v.reduce_max(negm, lg, axis=AX, negate=True),
                     [("Tm", tmm)], "Vm")
                vv = b.c("Vm")
                ms["pbdl_V"] = vv
                b.op("s", lambda s: s.activation(q_sb, lg, AF.Exp,
                                                 bias=negm[:, 0:1]),
                     [("Vm", vv)], "Sm")
                ms["pbdl_S"] = b.c("Sm")
                sq = b.c("Sm")
                b.op("v", lambda v: v.tensor_copy(q_cur, q_sb), [("Sm", sq)], "Vm")
                for k in range(8):
                    b.op("v", lambda v: v.reduce_max(mk, q_cur, axis=AX), [], "Vm")
                    if k == 0:
                        b.op("v", lambda v: v.tensor_copy(accs, mk), [], "Vm")
                    else:
                        b.op("v", lambda v: v.tensor_add(accs, accs, mk), [], "Vm")
                    b.op("v", lambda v: v.tensor_single_scalar(msk, q_cur,
                                                               mk[:, 0:1],
                                                               ALU.is_ge), [], "Vm")
                    b.op("v", lambda v: v.scalar_tensor_tensor(q_cur, msk, -2.0,
                                                               q_cur, ALU.mult,
                                                               ALU.add), [], "Vm")
                b.op("v", lambda v: v.reciprocal(rcp, accs), [], "Vm")
                b.op("v", lambda v: v.tensor_sub(selw, q_sb, q_cur), [], "Vm")
                b.op("v", lambda v: v.tensor_scalar(selw, selw, 0.5, None,
                                                    ALU.mult), [], "Vm")
                b.op("v", lambda v: v.tensor_scalar_mul(w_t, q_sb, rcp[:, 0:1]),
                     [], "Vm")
                b.op("v", lambda v: v.tensor_mul(w_t, w_t, selw), [], "Vm")
                vw = b.c("Vm")
                b.op("t", lambda t: t.transpose(pb_sm[0:32, 256:384], w_t,
                                                ident),
                     [("Vm", vw), ("Sm", ms.get("wT_evac", 0))], "Tm")
                twt = b.c("Tm")
                b.op("s", lambda s: s.copy(wT, pb_sm[0:32, 256:384]),
                     [("Tm", twt), ("RS", ms.get("wT_rs", 0))], "Sm")
                ms["wT_evac"] = b.c("Sm")
                swt = b.c("Sm")
                b.op("y", lambda y, d=rows_d[32:64, tt * 128:(tt + 1) * 128]:
                     y.dma_start(d, wT), [("Sm", swt)], "RS", 16)
                ms["wT_rs"] = b.c("RS")
        ms[f"xnT_done_{chunk_tag}"] = b.c("Vm")
        ms[f"xp_done_{chunk_tag}"] = b.c("Sm")
        if p1_level < 7:
            return
        sx = b.c("Sm")
        if router:
            b.op("y", lambda y: y.dma_start(rows_d[0:32, :], xpT[64:96, :]),
                 [("Sm", sx), ("BR", b.c("BR")), ("BW", b.c("BW"))], "RS", 16)
        else:
            sub = int(chunk_tag[1])
            b.op("v", lambda v, d=xpT_p[sub]: v.tensor_copy(d, xpT),
                 [("Sm", sx)], "Vm")
            ms[f"xpp_ready_{chunk_tag}"] = b.c("Vm")
            ms["xpT_vfree"] = b.c("Vm")
            b.op("y", lambda y, r0=64 + 16 * sub:
                 y.dma_start(rows_d[r0:r0 + 16, :], xpT[64:80, :]),
                 [("Sm", sx), ("BR", b.c("BR"))], "RS", 16)
            vx = b.c("Vm")
            b.op("y", lambda y, d=xnp_d[sub]: y.dma_start(d, xnT),
                 [("Vm", vx)], "XNP", 16)
        ms[f"rows_{chunk_tag}"] = b.c("RS")

    def mamba_prologue(chunk_tag, dt):
        if chunk_tag[0] == "p":
            xp_src = xpT_p[int(chunk_tag[1])]
        else:
            xp_src = xpT
        for hf in range(2):
            mm(pb_dl, wdtT[:, dt * 128:(dt + 1) * 128],
               xp_src[0:64, hf * 512:(hf + 1) * 512], True, True,
               [("Sm", ms.get("pbdl_S", 0)), ("Vm", ms.get("pbdl_V", 0)),
                ("PRG", b.c("PRG")),
                ("Sm", ms.get(f"xp_done_{chunk_tag}", 0)),
                ("Vm", ms.get(f"xpp_ready_{chunk_tag}", 0))], "Tm")
            tdl = b.c("Tm")
            b.op("s", lambda s, bia=bdt_sb[:, dt:dt + 1]:
                 s.activation(dscr, pb_dl, AF.Exp, bias=bia),
                 [("Tm", tdl), ("Vm", ms.get("dl_war", 0)),
                  ("OUT", ms.get("out_dma", 0)),
                  ("Vm", ms.get("out_tile_vfree", 0))], "Sm")
            ms["pbdl_S"] = b.c("Sm")
            b.op("s", lambda s, d=delta[:, hf * 512:(hf + 1) * 512]:
                 s.activation(d, dscr, AF.Ln, bias=cst[:, 1:2]), [], "Sm")
        sdl = b.c("Sm")
        if chunk_tag[0] == "p":
            sub = int(chunk_tag[1])
            b.op("y", lambda y, s_=xnp_d[sub][:, dt * C:(dt + 1) * C]:
                 y.dma_start(dxt, s_),
                 [("Vm", b.c("Vm")), ("XNP", 16 * (sub + 1))], "XT", 16)
            xl = b.c("XT")
            b.op("v", lambda v: v.tensor_mul(dxt, delta, dxt),
                 [("Sm", sdl), ("XT", xl)], "Vm")
        else:
            b.op("v", lambda v, xs=xnT[:, dt * C:(dt + 1) * C]:
                 v.tensor_mul(dxt, delta, xs), [("Sm", sdl)], "Vm")
        ms["dl_war"] = b.c("Vm")

    def mamba_nstep(chunk_tag, dt, n, do_y):
        if True:
            ci = dt * NST + n
            rb = 0 if chunk_tag[0] == "c" else 64 + 16 * int(chunk_tag[1])
            sl_ = n % 2
            b.op("y", lambda y, n_=rb + n, s2=sl_:
                 y.dma_start(B_sb[:, s2 * C:(s2 + 1) * C],
                             rows_d[n_:n_ + 1, :].broadcast_to([P, C])),
                 [("Vm", ms.get(f"Bw{sl_}", 0)),
                  ("RS", ms.get(f"rows_{chunk_tag}", 0))], "BR", 16)
            br = b.c("BR")
            if do_y:
                b.op("y", lambda y, n_=16 + n, s2=sl_:
                     y.dma_start(C_sb[:, s2 * C:(s2 + 1) * C],
                                 rows_d[n_:n_ + 1, :].broadcast_to([P, C])),
                     [("Vm", ms.get(f"Cw{sl_}", 0)),
                      ("RS", ms.get(f"rows_{chunk_tag}", 0))], "BW", 16)
            brc = b.c("BW")
            b.op("s", lambda s, i=ci, s2=sl_: s.activation(
                abar[:, s2 * C:(s2 + 1) * C], delta,
                AF.Exp, scale=A_sb[:, i:i + 1]),
                 [("Vm", ms.get(f"abar_war{sl_}", 0))], "Sm")
            sA = b.c("Sm")
            b.op("v", lambda v, s2=sl_: v.tensor_mul(
                bx, dxt, B_sb[:, s2 * C:(s2 + 1) * C]), [("BR", br)], "Vm")
            ms[f"Bw{sl_}"] = b.c("Vm")
            b.op("v", lambda v, i=ci, s2=sl_: v.tensor_tensor_scan(
                hsc, abar[:, s2 * C:(s2 + 1) * C], bx, carry[:, i:i + 1],
                ALU.mult, ALU.add), [("Sm", sA)], "Vm")
            ms[f"abar_war{sl_}"] = b.c("Vm")
            b.op("v", lambda v, i=ci: v.tensor_copy(carry[:, i:i + 1],
                                                    hsc[:, C - 1:C]), [], "Vm")
            if do_y:
                if n == 0:
                    b.op("v", lambda v, s2=sl_: v.tensor_mul(
                        ybuf, hsc, C_sb[:, s2 * C:(s2 + 1) * C]),
                        [("BW", brc)], "Vm")
                else:
                    b.op("v", lambda v, s2=sl_: v.tensor_mul(
                        bx, hsc, C_sb[:, s2 * C:(s2 + 1) * C]),
                        [("BW", brc)], "Vm")
                    b.op("v", lambda v: v.tensor_add(ybuf, ybuf, bx), [], "Vm")
                ms[f"Cw{sl_}"] = b.c("Vm")

    def mamba_epilogue(dt):
        b.op("v", lambda v, xs=xnT[:, dt * C:(dt + 1) * C],
             dc_=dsk_sb[:, dt:dt + 1]: v.scalar_tensor_tensor(
            ybuf, xs, dc_, ybuf, ALU.mult, ALU.add), [], "Vm")
        b.op("v", lambda v, d=acc[:, dt * C:(dt + 1) * C]:
             v.tensor_add(d, d, ybuf), [], "Vm")
        ms["mamba_v"] = b.c("Vm")

    def mamba_units(chunk_tag, dt, do_y):
        u = [lambda: mamba_prologue(chunk_tag, dt)]
        for n in range(NST):
            u.append(lambda n=n: mamba_nstep(chunk_tag, dt, n, do_y))
        if do_y:
            u.append(lambda: mamba_epilogue(dt))
        return u

    def mamba_dt(chunk_tag, dt, do_y):
        for u in mamba_units(chunk_tag, dt, do_y):
            u()

    wload_idx = {"wg": 0, "wu": 0, "wd": 0}

    def load_gu_col(nm, dr, slots, e, ic):
        k = wload_idx[nm]
        slot = k % 2
        wload_idx[nm] = k + 1
        sem = ("WG" if nm == "wg" else "WU") + str(slot)
        freekey = f"free_{nm}{slot}"
        b.op("g", lambda g, s_=slots[slot], a=dr[e, ic]: g.dma_start(s_, a),
             [("Tm", ms.get(freekey, 0))], sem, 16)
        return slot, b.c(sem)

    def load_wd_col(e, hc):
        k = wload_idx["wd"]
        slot = k % 3
        wload_idx["wd"] = k + 1
        sem = f"WD{slot}"
        b.op("g", lambda g, s_=wds[slot],
             a=wd_d[e, hc // 2, :, (hc % 2) * 768:(hc % 2) * 768 + 768]:
             g.dma_start(s_, a),
             [("Tm", ms.get(f"free_wd{slot}", 0))], sem, 16)
        return slot, b.c(sem)

    def pair(chunk_tag, p_idx, extra=None):
        extra = list(extra or [])
        n_slots = 2 * ICT + 16
        ne = len(extra)
        drip_sched = []
        for i in range(n_slots):
            if i < 2 * ICT:
                drip_sched.append((ne * 3 * (i + 1)) // (10 * 2 * ICT))
            else:
                base = (ne * 3) // 10
                drip_sched.append(base + ((ne - base) * (i - 2 * ICT + 1))
                                  // 16)
        slot_i = [0]

        def drip():
            want = drip_sched[min(slot_i[0], n_slots - 1)]
            slot_i[0] += 1
            while extra and (len(drip_sched) - len(extra) >= 0):
                done = drip_sched[-1] - len(extra)
                if done < want:
                    extra.pop(0)()
                else:
                    break

        e0, e1 = 2 * p_idx, 2 * p_idx + 1
        xn_ok = ms.get(f"xnT_done_{chunk_tag}", 0)
        for ei, e in enumerate((e0, e1)):
            b.op("y", lambda y, e_=e, i=ei: y.dma_start(
                wb[:, i * C:(i + 1) * C],
                rows_d[32 + e_:33 + e_, :].broadcast_to([P, C])),
                [("Vm", ms.get("wb_war", 0)),
                 ("RS", ms.get(f"rows_{chunk_tag}", 0))], "BW", 16)
        swb = b.c("BW")
        for ei, e in enumerate((e0, e1)):
            for ic in range(ICT):
                gslot, gsem = load_gu_col("wg", wg_d, wgs, e, ic)
                uslot, usem = load_gu_col("wu", wu_d, wus, e, ic)
                for sl in range(2):
                    j2 = (ic * 2 + sl) % 2
                    pg = [pb_g, pb_b0][0]
                    pu = [pb_u, pb_b1][0]
                    for hc in range(16):
                        w_ = [(f"WG{gslot}", gsem), ("Vm", xn_ok)]
                        if hc == 0:
                            w_.append(("Sm", ms.get("g_evac0", 0)))
                        mm(pg[:], wgs[gslot][:, hc * 128:(hc + 1) * 128],
                           xnT[:, hc * C + sl * 512: hc * C + (sl + 1) * 512],
                           hc == 0, hc == 15, w_, "Tm" if hc == 15 else None)
                    tg = b.c("Tm")
                    if sl == 1:
                        ms[f"free_wg{gslot}"] = tg
                    for hc in range(16):
                        w_ = [(f"WU{uslot}", usem)]
                        if hc == 0:
                            w_.append(("Sm", ms.get("u_evac0", 0)))
                        mm(pu[:], wus[uslot][:, hc * 128:(hc + 1) * 128],
                           xnT[:, hc * C + sl * 512: hc * C + (sl + 1) * 512],
                           hc == 0, hc == 15, w_, "Tm" if hc == 15 else None)
                    tu = b.c("Tm")
                    if sl == 1:
                        ms[f"free_wu{uslot}"] = tu
                    b.op("s", lambda s, j=j2, p_=pg: s.copy(
                        g_sb[:, j * 512:(j + 1) * 512], p_[:]),
                        [("Tm", tg), ("Sm", ms.get(f"silu{j2}", 0))], "Sm")
                    ms["g_evac0"] = b.c("Sm")
                    b.op("s", lambda s, j=j2, p_=pu: s.copy(
                        u_sb[:, j * 512:(j + 1) * 512], p_[:]),
                        [("Tm", tu), ("Vm", ms.get(f"h1_{j2}", 0))], "Sm")
                    ms["u_evac0"] = b.c("Sm")
                    b.op("s", lambda s, j=j2: s.activation(
                        sg_sb[:, 0:512],
                        g_sb[:, j * 512:(j + 1) * 512], AF.Silu),
                        [("Vm", ms.get("sg_war", 0))], "Sm")
                    ms[f"silu{j2}"] = b.c("Sm")
                    ssl = b.c("Sm")
                    b.op("v", lambda v, j=j2: v.tensor_mul(
                        sg_sb[:, 0:512],
                        sg_sb[:, 0:512],
                        u_sb[:, j * 512:(j + 1) * 512]), [("Sm", ssl)], "Vm")
                    ms[f"h1_{j2}"] = b.c("Vm")
                    b.op("v", lambda v, j=j2, i=ei, c_=ic, s_=sl: v.tensor_mul(
                        harena[:, i * ICT * C + c_ * C + s_ * 512:
                               i * ICT * C + c_ * C + s_ * 512 + 512],
                        sg_sb[:, 0:512],
                        wb[:, i * C + s_ * 512: i * C + s_ * 512 + 512]),
                        [("BW", swb), ("Tm", ms.get("h_rdT", 0))], "Vm")
                    ms["sg_war"] = b.c("Vm")
                drip()
            ms[f"h_done{ei}"] = b.c("Vm")
        ms["wb_war"] = b.c("Vm")
        hdone = b.c("Vm")
        for hc in range(16):
            s0, d0 = load_wd_col(e0, hc)
            s1, d1 = load_wd_col(e1, hc)
            for sl in range(2):
                k = (hc * 2 + sl) % 2
                nmm = 0
                for ei, (e, ws_, dsem) in enumerate(
                        ((e0, s0, d0), (e1, s1, d1))):
                    for icc in range(ICT):
                        w_ = [(f"WD{ws_}", dsem), ("Vm", hdone)]
                        if nmm == 0:
                            w_.append(("Vm", ms.get(f"acc_evac{k}", 0)))
                        mm(pb_acc[k][:],
                           wds[ws_][:, icc * 128:(icc + 1) * 128],
                           harena[:, ei * ICT * C + icc * C + sl * 512:
                                  ei * ICT * C + icc * C + sl * 512 + 512],
                           nmm == 0, nmm == 11, w_,
                           "Tm" if nmm == 11 else None)
                        nmm += 1
                tdn = b.c("Tm")
                if sl == 1:
                    ms[f"free_wd{s0}"] = tdn
                    ms[f"free_wd{s1}"] = tdn
                b.op("v", lambda v, h_=hc, s_=sl, k_=k: v.tensor_add(
                    acc[:, h_ * C + s_ * 512: h_ * C + s_ * 512 + 512],
                    acc[:, h_ * C + s_ * 512: h_ * C + s_ * 512 + 512],
                    pb_acc[k_][:]), [("Tm", tdn)], "Vm")
                ms[f"acc_evac{k}"] = b.c("Vm")
            drip()
        while extra:
            extra.pop(0)()
        ms["h_rdT"] = b.c("Tm")

    def out_assembly(c):
        vacc = b.c("Vm")
        s_dscr = b.c("Sm")
        for tt in range(TT):
            b.op("y", lambda y, t_=tt: y.dma_start(
                x_tile, xm_d[c * C + t_ * 128: c * C + (t_ + 1) * 128, :]),
                [("Vm", ms.get("x_tile_free", 0)),
                 ("Tm", ms.get("x_tile_freeT", 0))], "XT", 16)
            xt = b.c("XT")
            for half in range(2):
                for d8 in range(8):
                    dc = half * 8 + d8
                    sl = pb_bc[d8 % 2][:, 0:128]
                    b.op("t", lambda t, o=sl,
                         i_=acc[:, dc * C + tt * 128: dc * C + tt * 128 + 128]:
                         t.transpose(o, i_, ident),
                         [("Vm", vacc), ("Vm", ms.get(f"bcV{d8 % 2}", 0)),
                          ("Sm", ms.get(f"bcS{d8 % 2}", 0))], "Tm")
                    ttp = b.c("Tm")
                    b.op("v", lambda v, s_=sl, d8_=d8, dc_=dc: v.tensor_add(
                        out_tile[:, d8_ * 128:(d8_ + 1) * 128], s_,
                        x_tile[:, dc_ * 128:(dc_ + 1) * 128]),
                        [("Tm", ttp), ("XT", xt), ("Sm", s_dscr),
                         ("OUT", ms.get("out_dma", 0))], "Vm")
                    ms[f"bcV{d8 % 2}"] = b.c("Vm")
                vout = b.c("Vm")
                b.op("y", lambda y, t_=tt, h_=half: y.dma_start(
                    out_d[c * C + t_ * 128: c * C + (t_ + 1) * 128,
                          h_ * 1024:(h_ + 1) * 1024], out_tile),
                    [("Vm", vout)], "OUT", 16)
                ms["out_dma"] = b.c("OUT")
            ms["x_tile_free"] = b.c("Vm")
            ms["x_tile_freeT"] = b.c("Tm")
            ms["out_tile_vfree"] = b.c("Vm")

    startup()
    if do_p0:
        for sub in range(2):
            p1(f"p{sub}", lambda tt, s_=sub: xp_d[s_ * C + tt * 128:
                                                  s_ * C + (tt + 1) * 128, :],
               False)
    for c in range(2):
        tag = f"c{c}"
        b.op("v", lambda v: v.memset(acc, 0.0),
             [("Tm", ms.get("x_tile_freeT", 0))], "Vm")
        if do_p1:
            p1(tag, lambda tt, c_=c: xm_d[c_ * C + tt * 128:
                                          c_ * C + (tt + 1) * 128, :], do_router)
        if c == 0 and do_p0 and do_mamba and os.environ.get("KP0W") != "1":
            for dt in range(DT):
                mamba_dt("p0", dt, False)
                mamba_dt("p1", dt, False)
        for p_idx in range(16):
            extra = []
            if c == 0 and do_p0 and do_mamba and os.environ.get("KP0W") == "1":
                extra += mamba_units("p0", p_idx, False)
                extra += mamba_units("p1", p_idx, False)
            if do_mamba and os.environ.get("KP0ONLY") != "1":
                extra += mamba_units(tag, p_idx, True)
            if do_ffn:
                pair(tag, p_idx, extra)
            else:
                for u in extra:
                    u()
        if do_out:
            out_assembly(c)
    if _dbg:
        b.op("y", lambda y: y.dma_start(dbgc_d, carry),
             [("Vm", b.c("Vm"))], "OUT", 16)
    b.op("y", lambda y: y.nop(), [("OUT", b.c("OUT"))])

    sems = {}
    for s_ in b.cnt:
        sems[s_] = es.enter_context(nc.semaphore(s_))

    def emit(section, eng_key):
        waited = {}
        for (waits, fn, inc, inc_n) in b.ev[eng_key]:
            for sn, v in waits:
                if v > waited.get(sn, 0):
                    section.wait_ge(sems[sn], v)
                    waited[sn] = v
            inst = fn(section)
            if inc is not None:
                inst.then_inc(sems[inc], inc_n)

    with es, nc.Block() as block:
        @block.sync
        def _(sync):
            emit(sync, "y")

        @block.gpsimd
        def _(gpsimd):
            emit(gpsimd, "g")

        @block.tensor
        def _(tensor):
            emit(tensor, "t")

        @block.scalar
        def _(scalar):
            emit(scalar, "s")

        @block.vector
        def _(vector):
            emit(vector, "v")

    return nc


_NC_CACHE = None


def _get_nc():
    global _NC_CACHE
    if _NC_CACHE is None:
        _NC_CACHE = build_nc()
    return _NC_CACHE


def kernel(L0_attn_x, L0_attn_A_log, L0_attn_D, L0_attn_W_x, L0_attn_W_dt,
           L0_attn_b_dt, L0_ffn_router_w, L0_ffn_w_gate, L0_ffn_w_up,
           L0_ffn_w_down):
    x = np.ascontiguousarray(np.asarray(L0_attn_x, dtype=np.float32))
    A_log = np.asarray(L0_attn_A_log, np.float32)
    Dsk = np.asarray(L0_attn_D, np.float32)
    W_x = np.asarray(L0_attn_W_x, np.float32)
    W_dt = np.asarray(L0_attn_W_dt, np.float32)
    b_dt = np.asarray(L0_attn_b_dt, np.float32)
    rw = np.asarray(L0_ffn_router_w, np.float32)
    wg = np.asarray(L0_ffn_w_gate, np.float32)
    wu = np.asarray(L0_ffn_w_up, np.float32)
    wd = np.asarray(L0_ffn_w_down, np.float32)

    wg_r = np.ascontiguousarray(
        wg.reshape(E, 16, 128, ICT, 128).transpose(0, 3, 2, 1, 4)
    ).reshape(E, ICT, 128, 2048)
    wu_r = np.ascontiguousarray(
        wu.reshape(E, 16, 128, ICT, 128).transpose(0, 3, 2, 1, 4)
    ).reshape(E, ICT, 128, 2048)
    wd_r = np.ascontiguousarray(
        wd.reshape(E, ICT, 128, 16, 128).transpose(0, 3, 2, 1, 4)
    )
    wd_r = np.ascontiguousarray(
        wd_r.reshape(E, 8, 2, 128, 768).transpose(0, 1, 3, 2, 4)
    ).reshape(E, 8, 128, 1536)
    wxT = np.ascontiguousarray(
        W_x.T.reshape(16, 128, 96).transpose(1, 0, 2)).reshape(128, 16 * 96)
    rwT = np.ascontiguousarray(
        rw.T.reshape(16, 128, 32).transpose(1, 0, 2)).reshape(128, 16 * 32)
    wdtT = np.ascontiguousarray(W_dt.T)
    alog = np.ascontiguousarray(
        A_log.reshape(16, 128, 16).transpose(1, 0, 2)).reshape(128, 256)
    bdt = np.ascontiguousarray(b_dt.reshape(16, 128).T)
    dsk = np.ascontiguousarray(Dsk.reshape(16, 128).T)
    ident = np.eye(128, dtype=np.float32)
    onb = np.ones((1, 128), dtype=ml_dtypes.bfloat16)

    shared = dict(wg=wg_r, wu=wu_r, wd=wd_r, wxT=wxT, rwT=rwT, wdtT=wdtT,
                  alog=alog, bdt=bdt, dsk=dsk, idn=ident, onb=onb)
    zeros = np.zeros((2048, 2048), np.float32)
    in_maps = []
    for core in range(8):
        bb, half = core // 2, core % 2
        xm = np.ascontiguousarray(x[bb, half * 2048:(half + 1) * 2048])
        xq = zeros if half == 0 else np.ascontiguousarray(x[bb, 0:2048])
        in_maps.append(dict(xm=xm, xq=xq, **shared))

    nc = _get_nc()
    res = run_bass_kernel_spmd(nc, in_maps, core_ids=list(range(8)))
    outs = [res.results[i]["out"] for i in range(8)]
    full = np.concatenate(outs, axis=0).reshape(4, 4096, 2048)
    return full
